# version 11
# baseline (speedup 1.0000x reference)
import numpy as np
from contextlib import ExitStack

import concourse.bass as bass
import concourse.mybir as mybir
from concourse.bass_utils import run_bass_kernel_spmd

F32 = mybir.dt.float32
BF16 = mybir.dt.bfloat16
I32 = mybir.dt.int32
AF = mybir.ActivationFunctionType
ALU = mybir.AluOpType

NCORES = 8
S = 2048
D = 1024
NCH = 8
NT = 4
NKT = 16
H = 16
HD = 64
XNP = 8
XNW = S + XNP

ENGINES = ("tensor", "vector", "scalar", "gpsimd", "sync")


class Buf:
    __slots__ = ("w", "r")

    def __init__(self, seed=()):
        self.w = list(seed)
        self.r = []


DMA_LANES = 4


class Prog:
    def __init__(self):
        self.items = {e: [] for e in ENGINES}
        self.eng_n = {e: 0 for e in ENGINES}
        self.dma_n = {e: 0 for e in ENGINES}
        self.waited = {e: {} for e in ENGINES}

    def op(self, eng, emit, reads=(), writes=(), dma=False):
        deps = []
        for b in reads:
            deps += b.w
        for b in writes:
            deps += b.w
            deps += b.r
        if dma:
            j = self.dma_n[eng]
            lane = j % DMA_LANES
            if j >= DMA_LANES:
                deps = deps + [(("d", eng, lane), 16 * (j // DMA_LANES))]
            self.dma_n[eng] += 1
            tok = (("d", eng, lane), 16 * (j // DMA_LANES + 1))
        else:
            self.eng_n[eng] += 1
            tok = (("e", eng), self.eng_n[eng])
        waits = {}
        for key, val in deps:
            if val > self.waited[eng].get(key, 0) and val > waits.get(key, 0):
                waits[key] = val
        for k, v in waits.items():
            self.waited[eng][k] = v
        self.items[eng].append((waits, emit, tok))
        for b in reads:
            b.r.append(tok)
        for b in writes:
            b.w = [tok]
            b.r = []
        return tok

    def all_tokens(self):
        toks = []
        for e in ENGINES:
            if self.eng_n[e]:
                toks.append((("e", e), self.eng_n[e]))
            for lane in range(DMA_LANES):
                n = (self.dma_n[e] + DMA_LANES - 1 - lane) // DMA_LANES
                if n:
                    toks.append((("d", e, lane), 16 * n))
        return toks

    def emit(self, nc, es):
        sems = {}
        for e in ENGINES:
            sems[("e", e)] = es.enter_context(nc.semaphore(f"se_{e}"))
            for lane in range(DMA_LANES):
                sems[("d", e, lane)] = es.enter_context(
                    nc.semaphore(f"sd_{e}{lane}"))
        block = es.enter_context(nc.Block())

        def mk(eng):
            def run(handle):
                for waits, emit, tok in self.items[eng]:
                    for key, val in waits.items():
                        handle.wait_ge(sems[key], val)
                    ins = emit(handle)
                    key, _ = tok
                    ins.then_inc(sems[key], 16 if key[0] == "d" else 1)
                for lane in range(DMA_LANES):
                    n = (self.dma_n[eng] + DMA_LANES - 1 - lane) // DMA_LANES
                    if n:
                        handle.wait_ge(sems[("d", eng, lane)], 16 * n)
            return run

        runs = {e: mk(e) for e in ENGINES}
        block.tensor(runs["tensor"])
        block.vector(runs["vector"])
        block.scalar(runs["scalar"])
        block.gpsimd(runs["gpsimd"])
        block.sync(runs["sync"])


class CT:

    def __init__(self, t, nb=1, seed=()):
        self.t = t
        self.bufs = [Buf(seed) for _ in range(nb)]

    def __getitem__(self, i):
        return self.bufs[i]

    @property
    def b(self):
        return self.bufs[0]


def build_program(n_layers=4):
    nc = bass.Bass("TRN2", target_bir_lowering=False)
    p = Prog()

    def din(name, shape, dt=F32):
        return nc.dram_tensor(name, shape, dt, kind="ExternalInput").ap()

    x_in = din("x_cm", [2, D, S])
    y_out = nc.dram_tensor("y_cm", [2, D, S], F32, kind="ExternalOutput").ap()

    W_qkv0 = din("W_qkv0", [D, 3 * D])
    bqkv0 = din("bqkv0", [128, 24])
    vconv = din("vconv", [128, 24])
    ffnW = {}
    for li in (0, 1):
        for nm in ("r", "k", "v"):
            ffnW[(li, nm)] = din(f"ffn{li}_W{nm}T", [D, D])
        ffnW[(li, "mur")] = din(f"ffn{li}_mur", [128, 8])
        ffnW[(li, "muk")] = din(f"ffn{li}_muk", [128, 8])
    Wk1 = din("Wk1", [D, D])
    Wv1 = din("Wv1", [D, D])
    bk1 = din("bk1", [128, 8])
    bv1 = din("bv1", [128, 8])
    RHmat = din("RHmat", [D, H])
    bidir = din("bidir", [1, 128])
    selHP = din("selHP", [H, 1024])
    Wqkv2 = din("Wqkv2", [D, 3 * D])
    Wout2 = din("Wout2", [D, D])
    bout2 = din("bout2", [128, 8])
    cosA_d = din("cosA", [128, S])
    sinA_d = din("sinA", [128, S])
    Pswap_d = din("Pswap", [128, 128])
    wgT2 = din("wgT2", [D, 3 * D], I32)
    wuT2 = din("wuT2", [D, 3 * D], I32)
    wdT2 = din("wdT2", [3 * D, D], I32)
    wqT3 = din("wqT3", [D, D])
    wkT3 = din("wkT3", [D, D])
    wvT3 = din("wvT3", [D, D])
    woT3 = din("woT3", [D, D])
    bq3 = din("bq3", [128, 8])
    bk3 = din("bk3", [128, 8])
    bo3 = din("bo3", [128, 8])
    bv3row = din("bv3row", [1, D])
    T2 = din("T2", [H, HD, 4095])
    wg3T = din("wg3T", [D, 4 * D])
    wu3T = din("wu3T", [D, 4 * D])
    wd3T = din("wd3T", [4 * D, D])

    RELW = 2560
    relscr = nc.dram_tensor("relscr", [S, RELW], BF16)
    wg3q = nc.dram_tensor("wg3q", [D, 4 * D], BF16).ap()
    s3scr = nc.dram_tensor("s3scr", [D, S], BF16).ap()
    wu3q = nc.dram_tensor("wu3q", [D, 4 * D], BF16).ap()
    wd3q = nc.dram_tensor("wd3q", [4 * D, D], BF16).ap()

    es = ExitStack()

    def sb(name, shape, dt):
        return es.enter_context(nc.sbuf_tensor(name, shape, dt))

    XR = CT(sb("XR", [128, NCH * S], BF16), NCH)
    XN = CT(sb("XN", [128, NCH * XNW], BF16), NCH)
    ident = CT(sb("ident", [128, 128], BF16))
    MS = CT(sb("MS", [128, 512], BF16))
    DS = CT(sb("DS", [128, 512], BF16))
    MS2 = CT(sb("MS2", [128, 1024], BF16))
    ones_c128 = CT(sb("ones_c128", [128, 1], BF16))
    ones_c128f = CT(sb("ones_c128f", [128, 1], F32))
    ones_r128f = CT(sb("ones_r128f", [1, 128], F32))
    STAT = CT(sb("STAT", [1, 6144], BF16))
    ones_r128b = CT(sb("ones_r128b", [1, 128], BF16))
    SMALL = CT(sb("SMALL", [128, 256], F32), 8)
    SM2 = CT(sb("SM2", [128, 192], F32), 8)
    WST = CT(sb("WST", [128, 1024], F32), 2)

    _acnt = [0]
    RELQ = CT(None, 1)

    def anew(stack, name, shape, dt, nb=1):
        _acnt[0] += 1
        t = stack.enter_context(nc.sbuf_tensor(f"{name}{_acnt[0]}", shape, dt))
        return CT(t, nb, p.all_tokens())

    PS = [CT(es.enter_context(nc.psum_tensor(f"ps{i}", [128, 512], F32)))
          for i in range(7)]
    PT = CT(es.enter_context(nc.psum_tensor("psT", [128, 512], BF16)))

    _ROWS = [(6, 0), (6, 32), (6, 64), (5, 0), (5, 32), (5, 64), (4, 0), (4, 32)]

    def prow(i, c0=0, n=512):
        bank, part = _ROWS[i]
        return PS[bank], PS[bank].t[part:part + 1, c0:c0 + n]

    def dma(eng, out_ap, in_ap, reads=(), writes=()):
        return p.op(eng, lambda h, o=out_ap, i=in_ap: h.dma_start(out=o, in_=i),
                    reads=reads, writes=writes, dma=True)

    def mm(out_ap, lhsT_ap, rhs_ap, reads, wbuf, start, stop):
        skip = out_ap.shape[0] < 128
        return p.op("tensor",
                    lambda h, o=out_ap, l=lhsT_ap, r=rhs_ap, s=start, t=stop,
                    k=skip:
                    h.matmul(o, l, r, start=s, stop=t, skip_group_check=k),
                    reads=reads, writes=[wbuf])

    def act(out_ap, in_ap, func, reads, writes, bias=0.0, scale=1.0, accum=None):
        if func == AF.Copy and not isinstance(bias, float):
            func = AF.Identity
        return p.op("scalar",
                    lambda h, o=out_ap, i=in_ap, f=func, b=bias, s=scale, a=accum:
                    h.activation(o, i, f, bias=b, scale=s, accum_out=a),
                    reads=reads, writes=writes)

    def tt(op_, out_ap, a_ap, b_ap, reads, writes, eng="vector"):
        return p.op(eng,
                    lambda h, o=out_ap, a=a_ap, b=b_ap, oo=op_:
                    h.tensor_tensor(o, a, b, oo),
                    reads=reads, writes=writes)

    def ts(out_ap, in_ap, s1, reads, writes, op0=ALU.mult, s2=None, op1=None,
           accum=None, eng="vector"):
        def emit(h, o=out_ap, i=in_ap, a=s1, b=s2, o0=op0, o1=op1, ac=accum):
            if o1 is None:
                return h.tensor_scalar(o, i, a, b, o0, accum_out=ac)
            return h.tensor_scalar(o, i, a, b, o0, op1=o1, accum_out=ac)
        return p.op(eng, emit, reads=reads, writes=writes)

    def stt(out_ap, in0_ap, scal, in1_ap, op0, op1, reads, writes, accum=None):
        return p.op("vector",
                    lambda h, o=out_ap, i=in0_ap, s=scal, j=in1_ap, a=op0, b=op1,
                    ac=accum:
                    h.scalar_tensor_tensor(o, i, s, j, a, b, accum_out=ac),
                    reads=reads, writes=writes)

    def recip(out_ap, in_ap, reads, writes):
        def emit(h, o=out_ap, i=in_ap):
            with nc.allow_low_precision(reason="bf16 stat rows"):
                return h.reciprocal(o, i)
        return p.op("vector", emit, reads=reads, writes=writes)

    def copy(out_ap, in_ap, reads, writes, eng="vector"):
        return p.op(eng, lambda h, o=out_ap, i=in_ap: h.tensor_copy(o, i),
                    reads=reads, writes=writes)

    def memset(out_ap, val, writes, eng="vector"):
        return p.op(eng, lambda h, o=out_ap, v=val: h.memset(o, v), writes=writes)

    def scan(out_ap, d0_ap, d1_ap, init, op0, op1, reads, writes):
        return p.op("vector",
                    lambda h, o=out_ap, a=d0_ap, b=d1_ap, i=init, x=op0, y=op1:
                    h.tensor_tensor_scan(o, a, b, i, x, y),
                    reads=reads, writes=writes)

    def transp(out_ap, in_ap, reads, writes):
        return p.op("tensor",
                    lambda h, o=out_ap, i=in_ap: h.transpose(o, i, ident.t[:, :]),
                    reads=reads + [ident.b], writes=writes)

    def xr(c, c0=0, n=S):
        return XR.t[:, c * S + c0: c * S + c0 + n]

    def xn(c, c0=0, n=S):
        return XN.t[:, c * XNW + XNP + c0: c * XNW + XNP + c0 + n]

    def sschunk(c, c0=0, n=S):
        return SS.t[:, c * S + c0: c * S + c0 + n]

    def aa(c, c0=0, n=S):
        return AA.t[:, c * S + c0: c * S + c0 + n]

    def bbc(c, c0=0, n=S):
        return BB.t[:, c * S + c0: c * S + c0 + n]

    def build_consts():
        memset(ident.t[:, :], 1.0, [ident.b], eng="gpsimd")
        p.op("gpsimd",
             lambda h: h.affine_select(out=ident.t[:, :], in_=ident.t[:, :],
                                       pattern=[[1, 128]], compare_op=ALU.is_equal,
                                       fill=0.0, base=0, channel_multiplier=-1),
             reads=[], writes=[ident.b])
        memset(MS.t[:, :], 1.0, [MS.b], eng="gpsimd")
        p.op("gpsimd",
             lambda h: h.affine_select(out=MS.t[:, :], in_=MS.t[:, :],
                                       pattern=[[1, 512]], compare_op=ALU.is_ge,
                                       fill=0.0, base=0, channel_multiplier=-1),
             reads=[], writes=[MS.b])
        memset(DS.t[:, :], 1.0, [DS.b], eng="gpsimd")
        p.op("gpsimd",
             lambda h: h.affine_select(out=DS.t[:, :], in_=DS.t[:, :],
                                       pattern=[[1, 512]], compare_op=ALU.is_equal,
                                       fill=0.0, base=0, channel_multiplier=-1),
             reads=[], writes=[DS.b])
        memset(MS2.t[:, :], 1.0, [MS2.b], eng="gpsimd")
        p.op("gpsimd",
             lambda h: h.affine_select(out=MS2.t[:, :], in_=MS2.t[:, :],
                                       pattern=[[1, 1024]], compare_op=ALU.is_ge,
                                       fill=0.0, base=-512, channel_multiplier=-1),
             reads=[], writes=[MS2.b])
        memset(ones_c128.t[:, :], 1.0, [ones_c128.b], eng="gpsimd")
        memset(ones_c128f.t[:, :], 1.0, [ones_c128f.b], eng="gpsimd")
        memset(ones_r128f.t[:, :], 1.0, [ones_r128f.b], eng="gpsimd")
        memset(ones_r128b.t[:, :], 1.0, [ones_r128b.b], eng="gpsimd")

    smalls = {}

    def load_small(name, dram_ap, slot, pool=SMALL, rows=128, cols=None):
        cols = dram_ap.shape[-1] if cols is None else cols
        t = pool.t[0:rows, slot * 32: slot * 32 + cols]
        dma("sync", t, dram_ap, writes=[pool[slot % 8]])
        smalls[name] = (t, pool[slot % 8])
        return t

    wst_state = {"i": 0}

    def load_w_cast(dst_ap, dst_buf, dram_ap, scale_col=None, rows=128):
        if dram_ap.dtype == BF16:
            dma("sync", dst_ap, dram_ap, reads=[RELQ.b], writes=[dst_buf])
            return
        i = wst_state["i"] % 2
        wst_state["i"] += 1
        ncols = dram_ap.shape[-1]
        assert ncols <= 512
        stg = WST.t[0:rows, i * 512: i * 512 + ncols]
        if dram_ap.dtype == I32:
            stg = stg.bitcast(I32)
        dma("sync", stg, dram_ap, writes=[WST[i]])
        if scale_col is None:
            copy(dst_ap, stg, [WST[i]], [dst_buf], eng="gpsimd")
        else:
            sc, scb = scale_col
            ts(dst_ap, stg, sc, [WST[i], scb], [dst_buf], eng="gpsimd")

    def layernorm(sq_t, sq_buf):
        for c in range(NCH):
            memset(XN.t[:, c * XNW: c * XNW + XNP], 0.0, [XN[c]])
        for c in range(NCH):
            for tci in range(NT):
                tt(ALU.mult, sq_t, xr(c, tci * 512, 512), xr(c, tci * 512, 512),
                   [XR[c]], [sq_buf])
                ct1, r1 = prow(tci)
                mm(r1, ones_c128.t[:, :],
                   xr(c, tci * 512, 512), [XR[c], ones_c128.b], ct1.b,
                   c == 0, c == NCH - 1)
                ct2, r2 = prow(4 + tci)
                mm(r2, ones_c128.t[:, :],
                   sq_t, [sq_buf, ones_c128.b],
                   ct2.b, c == 0, c == NCH - 1)
        for tci in range(NT):
            ct1, r1 = prow(tci)
            ct2, r2 = prow(4 + tci)
            act(srow("mean"), r1, AF.Copy, [ct1.b], [STAT.b], scale=1.0 / D)
            act(srow("rstd"), r2, AF.Copy, [ct2.b], [STAT.b], scale=1.0 / D)
            tt(ALU.mult, srow("tmp"), srow("mean"), srow("mean"),
               [STAT.b], [STAT.b])
            tt(ALU.subtract, srow("rstd"), srow("rstd"), srow("tmp"),
               [STAT.b], [STAT.b])
            ts(srow("rstd"), srow("rstd"), 1e-5, [STAT.b], [STAT.b],
               op0=ALU.add, s2=1e-20, op1=ALU.max)
            act(srow("tmp"), srow("rstd"), AF.Sqrt, [STAT.b], [STAT.b])
            recip(srow("rstd"), srow("tmp"), [STAT.b], [STAT.b])
            tt(ALU.mult, srow("mrs"), srow("mean"), srow("rstd"),
               [STAT.b], [STAT.b])
            mm(PS[0].t[:, :], ones_r128b.t[:, :],
               srow("rstd"), [STAT.b, ones_r128b.b], PS[0].b, True, True)
            mm(PS[1].t[:, :], ones_r128b.t[:, :],
               srow("mrs"), [STAT.b, ones_r128b.b], PS[1].b, True, True)
            for c in range(NCH):
                tt(ALU.mult, sq_t, xr(c, tci * 512, 512), PS[0].t[:, :],
                   [XR[c], PS[0].b], [sq_buf])
                tt(ALU.subtract, xn(c, tci * 512, 512), sq_t,
                   PS[1].t[:, :], [sq_buf, PS[1].b], [XN[c]])

    def proj(w_dram, m0, n_m, dst_cb, wpool, src_fn=None, src_bufs=None,
             mgroup=4, scale_col=None, shifted_pair=None, wpool2=None):
        if src_fn is None:
            src_fn = xn
            src_bufs = XN.bufs
        wslot = wpool.t.shape[1] // NCH
        hm = max(1, wslot // 256)
        hw = hm * 128
        halves = [(g0, min(hm, n_m - g0)) for g0 in range(0, n_m, hm)]
        wb = {}
        for c in range(NCH):
            for hf in range(2):
                wb[(c, hf, 0)] = Buf(wpool[c].w)
                if wpool2 is not None:
                    wb[(c, hf, 1)] = Buf(wpool2[c].w)

        def issue_loads(gi):
            g0, gn = halves[gi]
            hf = gi % 2
            for c in range(NCH):
                load_w_cast(
                    wpool.t[:, c * wslot + hf * hw: c * wslot + hf * hw + gn * 128],
                    wb[(c, hf, 0)],
                    w_dram[c * 128:(c + 1) * 128,
                           (m0 + g0) * 128:(m0 + g0 + gn) * 128],
                    scale_col=None if scale_col is None else scale_col(c))
                if shifted_pair is not None:
                    w2d, sc2 = shifted_pair
                    load_w_cast(
                        wpool2.t[:, c * wslot + hf * hw: c * wslot + hf * hw + gn * 128],
                        wb[(c, hf, 1)],
                        w2d[c * 128:(c + 1) * 128,
                            (m0 + g0) * 128:(m0 + g0 + gn) * 128],
                        scale_col=sc2(c))

        issue_loads(0)
        for gi, (g0, gn) in enumerate(halves):
            hf = gi % 2
            if gi + 1 < len(halves):
                issue_loads(gi + 1)
            for ml in range(gn):
                m = g0 + ml
                for c in range(NCH):
                    for tc in range(NT):
                        ps = PS[tc % 4]
                        last = (c == NCH - 1) and shifted_pair is None
                        mm(ps.t[:, :],
                           wpool.t[:, c * wslot + hf * hw + ml * 128:
                                   c * wslot + hf * hw + (ml + 1) * 128],
                           src_fn(c, tc * 512, 512),
                           [wb[(c, hf, 0)], src_bufs[c]],
                           ps.b, c == 0, last)
                if shifted_pair is not None:
                    for c in range(NCH):
                        for tc in range(NT):
                            ps = PS[tc % 4]
                            mm(ps.t[:, :],
                               wpool2.t[:, c * wslot + hf * hw + ml * 128:
                                       c * wslot + hf * hw + (ml + 1) * 128],
                               src_fn(c, tc * 512 - 1, 512),
                               [wb[(c, hf, 1)], src_bufs[c]],
                               ps.b, False, c == NCH - 1)
                for tc in range(NT):
                    dst_cb(m, tc, PS[tc % 4])
        for c in range(NCH):
            toks = []
            for hf in range(2):
                for wi in (0, 1):
                    if (c, hf, wi) in wb:
                        toks += wb[(c, hf, wi)].w + wb[(c, hf, wi)].r
            wpool[c].w = toks
            wpool[c].r = []
            if wpool2 is not None:
                wpool2[c].w = list(toks)
                wpool2[c].r = []

    cur = {}

    def aa(c, c0=0, n=S):
        return cur["AA"].t[:, c * S + c0: c * S + c0 + n]

    def bbc(c, c0=0, n=S):
        return cur["BB"].t[:, c * S + c0: c * S + c0 + n]

    def sschunk(c, c0=0, n=S):
        return cur["SS"].t[:, c * S + c0: c * S + c0 + n]

    def srow(name, c0=0, n=512):
        base = {"mean": 0, "rstd": 512, "tmp": 1024, "mrs": 1536,
                "rz": 2048, "diag": 4096}[name]
        return STAT.t[0:1, base + c0: base + c0 + n]

    def layer0(bi):
        esL = ExitStack()
        AA = anew(esL, "AA", [128, NCH * S], BF16, NCH)
        cur["SS"] = AA

        es2 = ExitStack()
        BB = anew(es2, "BB", [128, NCH * S], BF16, NCH)
        BBfine = CT(BB.t, NCH * NKT)
        E0 = anew(es2, "E0", [128, 17408], BF16, NKT)
        SQ = anew(es2, "SQ", [128, 512], BF16, 1)
        WQ = anew(es2, "WQ", [128, 8 * 256], BF16, NCH)
        cur["AA"], cur["BB"] = AA, BB
        eoff = [0] * NKT
        o = 0
        for kt in range(NKT):
            eoff[kt] = o
            o += S - 128 * kt
        assert o == 17408

        def e0sl(kt, qc):
            if kt >= 4 * qc:
                dmask = 128 * kt - 512 * qc
                w = 512 - dmask
                return (E0.t[:, eoff[kt]: eoff[kt] + w], w, dmask)
            off = 512 * qc - 128 * kt
            return (E0.t[:, eoff[kt] + off: eoff[kt] + off + 512], 512, 0)

        layernorm(SQ.t[:, 0:512], SQ[0])
        if KPHASE <= 1:
            es2.close(); esL.close(); return

        bq = smalls["bqkv0"][0]
        bqb = smalls["bqkv0"][1]
        ts(SM2.t[:, 0:8], bq[:, 0:8], 1.0 / 32.0, [bqb], [SM2[0]])

        def q_cb(m, tc, ps):
            act(aa(m, tc * 512, 512), ps.t[:, :], AF.Copy, [ps.b, SM2[0]],
                [AA[m]], bias=SM2.t[:, m:m + 1], scale=1.0 / 32.0)

        def k_cb(m, tc, ps):
            act(bbc(m, tc * 512, 512), ps.t[:, :], AF.Copy, [ps.b, bqb],
                [BB[m]], bias=bq[:, 8 + m:9 + m])

        proj(W_qkv0, 0, 8, q_cb, wpool=WQ, mgroup=2)
        proj(W_qkv0, 8, 8, k_cb, wpool=WQ, mgroup=2)
        if KPHASE <= 2:
            es2.close(); esL.close(); return

        l0items = [(kt, qc) for kt in range(NKT)
                   for qc in range(kt // 4, NT)]

        def l0_s(i):
            kt, qc = l0items[i]
            ps = PS[i % 3]
            for c in range(NCH):
                mm(ps.t[:, :], bbc(c, kt * 128, 128), aa(c, qc * 512, 512),
                   [AA[c], BB[c]], ps.b, c == 0, c == NCH - 1)
            dst, w, dmask = e0sl(kt, qc)
            act(dst, ps.t[:, dmask:512], AF.Exp, [ps.b], [E0[kt]])
            if kt >= 4 * qc:
                tt(ALU.mult, dst, dst, MS.t[:, 0:w],
                   [E0[kt], MS.b], [E0[kt]])
                tt(ALU.mult, SQ.t[:, 0:w], dst,
                   DS.t[:, 0:w], [E0[kt], DS.b], [SQ[0]])
                ctd, rd = prow(4 + qc, dmask, w)
                mm(rd, ones_c128.t[:, :],
                   SQ.t[:, 0:w], [SQ[0], ones_c128.b], ctd.b,
                   kt == 4 * qc, kt == 4 * qc + 3)

        def l0_z(i):
            kt, qc = l0items[i]
            dst, w, dmask = e0sl(kt, qc)
            ctz, rz = prow(qc, dmask, 512 - dmask)
            mm(rz, ones_c128.t[:, :], dst,
               [E0[kt], ones_c128.b], ctz.b, kt == 0, kt == 4 * qc + 3)

        for i in range(len(l0items) + 1):
            if i < len(l0items):
                l0_s(i)
            if i >= 1:
                l0_z(i - 1)

        for tc in range(NT):
            ct1, r1 = prow(tc)
            ct2, r2 = prow(4 + tc)
            act(srow("rz", tc * 512, 512), r1,
                AF.Copy, [ct1.b], [STAT.b])
            act(srow("diag", tc * 512, 512), r2,
                AF.Copy, [ct2.b], [STAT.b])
        recip(srow("rz", 0, 2048), srow("rz", 0, 2048), [STAT.b], [STAT.b])

        if KPHASE <= 3:
            es2.close(); esL.close(); return
        def v_cb(m, tc, ps):
            act(aa(m, tc * 512, 512), ps.t[:, :], AF.Copy, [ps.b, bqb],
                [AA[m]], bias=bq[:, 16 + m:17 + m])

        proj(W_qkv0, 16, 8, v_cb, wpool=WQ, mgroup=2)

        vc = smalls["vconv"][0]
        vcb = smalls["vconv"][1]
        for c in range(NCH):
            ts(bbc(c), aa(c), vc[:, c * 3 + 1:c * 3 + 2], [AA[c], vcb], [BB[c]])
            stt(bbc(c, 1, S - 1), aa(c, 0, S - 1), vc[:, c * 3:c * 3 + 1],
                bbc(c, 1, S - 1), ALU.mult, ALU.add,
                [AA[c], vcb, BB[c]], [BB[c]])
            stt(bbc(c, 0, S - 1), aa(c, 1, S - 1), vc[:, c * 3 + 2:c * 3 + 3],
                bbc(c, 0, S - 1), ALU.mult, ALU.add,
                [AA[c], vcb, BB[c]], [BB[c]])
        for c in range(NCH):
            ts(aa(c, 0, S - 1), aa(c, 1, S - 1), vc[:, c * 3 + 2:c * 3 + 3],
               [AA[c], vcb], [AA[c]])
            memset(AA.t[:, c * S + S - 1: c * S + S], 0.0, [AA[c]])
        for i in range(NCH * NKT):
            BBfine.bufs[i].w = list(BB[i // NKT].w)
            BBfine.bufs[i].r = []

        for c in range(NCH):
            for kt in range(NKT):
                fb = BBfine.bufs[c * NKT + kt]
                transp(PT.t[:, 0:128], bbc(c, kt * 128, 128), [fb], [PT.b])
                act(bbc(c, kt * 128, 128), PT.t[:, 0:128], AF.Copy,
                    [PT.b], [fb])

        for qc in range(NT):
            mm(PS[5].t[:, :], ones_r128b.t[:, :],
               srow("rz", qc * 512, 512), [STAT.b, ones_r128b.b],
               PS[5].b, True, True)
            act(WQ.t[:, 0:512], PS[5].t[:, :], AF.Copy, [PS[5].b], [WQ[0]])
            mm(PS[4].t[:, :], ones_r128b.t[:, :],
               srow("diag", qc * 512, 512), [STAT.b, ones_r128b.b],
               PS[4].b, True, True)
            for c in range(NCH):
                ps = PS[c % 3]
                nkt_q = 4 * qc + 4
                for kt in range(nkt_q):
                    fb = BBfine.bufs[c * NKT + kt]
                    src_e, w, dmask = e0sl(kt, qc)
                    mm(ps.t[:, dmask:512], bbc(c, kt * 128, 128), src_e,
                       [fb, E0[kt]], ps.b, kt == 0, kt == nkt_q - 1)
                tt(ALU.mult, SQ.t[:, 0:512], aa(c, qc * 512, 512), PS[4].t[:, :],
                   [AA[c], PS[4].b, SQ[0]], [SQ[0]])
                tt(ALU.subtract, SQ.t[:, 0:512], ps.t[:, :], SQ.t[:, 0:512],
                   [ps.b, SQ[0]], [SQ[0]])
                tt(ALU.mult, aa(c, qc * 512, 512), SQ.t[:, 0:512],
                   WQ.t[:, 0:512], [SQ[0], WQ[0], AA[c]], [AA[c]])

        if KPHASE <= 4:
            es2.close(); esL.close(); return
        es2.close()
        ffn_update(0)
        esL.close()

    def layer1(bi):
        esL = ExitStack()
        AA = anew(esL, "AA", [128, NCH * S], BF16, NCH)
        cur["SS"] = AA

        es2 = ExitStack()
        VV = anew(es2, "VV", [128, NCH * S], BF16, NCH)
        WQ = anew(es2, "WQ", [128, 8 * 512], BF16, NCH)
        SQ = anew(es2, "SQ", [128, 512], BF16, 1)
        DV = anew(es2, "DV", [16, S], BF16, 1)
        RHf = anew(es2, "RHf", [128, 128], F32, 1)
        SEL = anew(es2, "SEL", [16, 1024], BF16, 1)
        BIC = anew(es2, "BIC", [128, 128], F32, 1)
        WCOL = anew(es2, "WCOL", [128, 80], F32, 1)
        KRL = anew(es2, "KRL", [128, 512], F32, 1)
        VB = anew(es2, "VB", [128, S], BF16, 1)
        NB = anew(es2, "NB", [128, S], BF16, 1)
        VBW = anew(es2, "VBW", [128, S], BF16, 1)
        NBW = anew(es2, "NBW", [128, S], BF16, 1)
        D1 = anew(es2, "D1", [128, S], BF16, 1)
        HA = anew(es2, "HA", [128, S + 1], BF16, 1)
        HB = anew(es2, "HB", [128, S + 1], BF16, 1)
        ACC = anew(es2, "ACC", [128, S], F32, 1)
        COLS = anew(es2, "COLS", [128, 8], F32, 1)
        cur["AA"], cur["BB"] = AA, VV

        layernorm(SQ.t[:, :], SQ[0])
        if KPHASE == 11:
            es2.close(); esL.close(); return

        for piece in range(2):
            i = piece % 2
            dma("sync", WST.t[0:16, i * 512:i * 512 + 512],
                selHP[:, piece * 512:piece * 512 + 512], writes=[WST[i]])
            copy(SEL.t[:, piece * 512:piece * 512 + 512],
                 WST.t[0:16, i * 512:i * 512 + 512], [WST[i]], [SEL.b],
                 eng="gpsimd")
        dma("sync", WST.t[0:1, 0:128], bidir, writes=[WST[0]])
        mm(PS[6].t[:, 0:128], ones_r128f.t[:, :], WST.t[0:1, 0:128],
           [WST[0], ones_r128f.b], PS[6].b, True, True)
        copy(BIC.t[:, :], PS[6].t[:, 0:128], [PS[6].b], [BIC.b])
        for c in range(NCH):
            dma("sync", WST.t[:, (c % 2) * 512:(c % 2) * 512 + 16],
                RHmat[c * 128:(c + 1) * 128, :], writes=[WST[c % 2]])
            copy(RHf.t[:, c * 16:(c + 1) * 16],
                 WST.t[:, (c % 2) * 512:(c % 2) * 512 + 16], [WST[c % 2]],
                 [RHf.b], eng="gpsimd")
        for hp in range(8):
            for ri in range(8):
                copy(WCOL.t[0:64, hp * 8 + ri:hp * 8 + ri + 1],
                     BIC.t[0:64, (2 * hp) * 8 + ri:(2 * hp) * 8 + ri + 1],
                     [BIC.b], [WCOL.b])
                copy(WCOL.t[64:128, hp * 8 + ri:hp * 8 + ri + 1],
                     BIC.t[64:128, (2 * hp + 1) * 8 + ri:(2 * hp + 1) * 8 + ri + 1],
                     [BIC.b], [WCOL.b])
        for hp in range(8):
            copy(WCOL.t[:, 64 + hp:65 + hp], WCOL.t[:, hp * 8:hp * 8 + 1],
                 [WCOL.b], [WCOL.b])
            for ri in range(1, 8):
                tt(ALU.add, WCOL.t[:, 64 + hp:65 + hp],
                   WCOL.t[:, 64 + hp:65 + hp],
                   WCOL.t[:, hp * 8 + ri:hp * 8 + ri + 1], [WCOL.b], [WCOL.b])

        bk = smalls["bk1"][0]
        bkb = smalls["bk1"][1]
        bv = smalls["bv1"][0]
        bvb = smalls["bv1"][1]

        def v_cb(m, tc, ps):
            act(VV.t[:, m * S + tc * 512: m * S + tc * 512 + 512], ps.t[:, :],
                AF.Relu, [ps.b, bvb], [VV[m]], bias=bv[:, m:m + 1])

        proj(Wv1, 0, 8, v_cb, wpool=WQ)
        if KPHASE == 12:
            es2.close(); esL.close(); return

        _DOTS = [(4, 0), (4, 32), (4, 64), (5, 0)]

        def pdot(tc, c0=0, n=512):
            bank, part = _DOTS[tc]
            return PS[bank], PS[bank].t[part:part + 16, c0:c0 + n]

        for m in range(NCH):
            for c in range(NCH):
                load_w_cast(WQ.t[:, c * 512: c * 512 + 128], WQ[c],
                            Wk1[c * 128:(c + 1) * 128, m * 128:(m + 1) * 128])
            for tc in range(NT):
                ps = PS[tc % 3]
                for c in range(NCH):
                    mm(ps.t[:, :], WQ.t[:, c * 512: c * 512 + 128],
                       xn(c, tc * 512, 512), [WQ[c], XN[c]],
                       ps.b, c == 0, c == NCH - 1)
                act(KRL.t[:, :], ps.t[:, :], AF.Relu, [ps.b, bkb], [KRL.b],
                    bias=bk[:, m:m + 1])
                ctd, rd = pdot(tc)
                mm(rd, RHf.t[:, m * 16:(m + 1) * 16], KRL.t[:, :],
                   [RHf.b, KRL.b], ctd.b, m == 0, m == NCH - 1)
        for tc in range(NT):
            ctd, rd = pdot(tc)
            ts(DV.t[:, tc * 512:tc * 512 + 512], rd, 0.5, [ctd.b], [DV.b],
               op0=ALU.is_gt)

        if KPHASE == 13:
            es2.close(); esL.close(); return
        for hp in range(8):
            vv = lambda c0=0, n=S: VV.t[:, hp * S + c0: hp * S + c0 + n]
            for tc in range(NT):
                mm(PS[6].t[:, :], SEL.t[:, hp * 128:hp * 128 + 128],
                   DV.t[:, tc * 512:tc * 512 + 512], [SEL.b, DV.b], PS[6].b,
                   True, True)
                act(VB.t[:, tc * 512:tc * 512 + 512], PS[6].t[:, :], AF.Copy,
                    [PS[6].b], [VB.b])
            memset(VB.t[:, 0:1], 0.0, [VB.b])
            ts(NB.t[:, :], VB.t[:, :], -1.0, [VB.b], [NB.b], s2=1.0,
               op1=ALU.add)
            copy(VBW.t[:, :], VB.t[:, :], [VB.b], [VBW.b])
            memset(VBW.t[:, S - 1:S], 0.0, [VBW.b])
            ts(NBW.t[:, :], VBW.t[:, :], -1.0, [VBW.b], [NBW.b], s2=1.0,
               op1=ALU.add)
            memset(HA.t[:, 0:1], 0.0, [HA.b])
            memset(HA.t[:, S:S + 1], 0.0, [HA.b])
            memset(HB.t[:, 0:1], 0.0, [HB.b])
            memset(HB.t[:, S:S + 1], 0.0, [HB.b])

            wc = lambda ri: WCOL.t[:, hp * 8 + ri:hp * 8 + ri + 1]

            tt(ALU.mult, D1.t[:, :], VB.t[:, :], vv(), [VB.b, VV[hp]], [D1.b])
            scan(HA.t[:, 1:S + 1], NB.t[:, :], D1.t[:, :], vv(0, 1),
                 ALU.mult, ALU.add, [NB.b, D1.b, VV[hp]], [HA.b])
            ts(ACC.t[:, :], HA.t[:, 1:S + 1], wc(0), [HA.b, WCOL.b], [ACC.b])
            hcur, hnxt = HA, HB
            for ri in range(1, 4):
                tt(ALU.mult, D1.t[:, :], VB.t[:, :], hcur.t[:, 0:S],
                   [VB.b, hcur.b], [D1.b])
                scan(hnxt.t[:, 1:S + 1], NB.t[:, :], D1.t[:, :], vv(0, 1),
                     ALU.mult, ALU.add, [NB.b, D1.b, VV[hp]], [hnxt.b])
                stt(ACC.t[:, :], hnxt.t[:, 1:S + 1], wc(ri), ACC.t[:, :],
                    ALU.mult, ALU.add, [hnxt.b, WCOL.b, ACC.b], [ACC.b])
                hcur, hnxt = hnxt, hcur

            tt(ALU.mult, COLS.t[:, 0:1], VB.t[:, S - 1:S], vv(S - 1, 1),
               [VB.b, VV[hp]], [COLS.b])
            tt(ALU.mult, COLS.t[:, 1:2], NB.t[:, S - 1:S], vv(0, 1),
               [NB.b, VV[hp]], [COLS.b])
            tt(ALU.add, COLS.t[:, 0:1], COLS.t[:, 0:1], COLS.t[:, 1:2],
               [COLS.b], [COLS.b])
            tt(ALU.mult, D1.t[:, 0:S - 1], VBW.t[:, 0:S - 1], vv(1, S - 1),
               [VBW.b, VV[hp]], [D1.b])
            memset(D1.t[:, S - 1:S], 0.0, [D1.b])
            scan(HA.t[:, 0:S][:, ::-1], NBW.t[:, ::-1], D1.t[:, ::-1],
                 COLS.t[:, 0:1], ALU.mult, ALU.add,
                 [NBW.b, D1.b, COLS.b], [HA.b])
            stt(ACC.t[:, :], HA.t[:, 0:S], wc(4), ACC.t[:, :],
                ALU.mult, ALU.add, [HA.b, WCOL.b, ACC.b], [ACC.b])
            hcur, hnxt = HA, HB
            for ri in range(1, 4):
                tt(ALU.mult, D1.t[:, :], VBW.t[:, :], hcur.t[:, 1:S + 1],
                   [VBW.b, hcur.b], [D1.b])
                scan(hnxt.t[:, 0:S][:, ::-1], NBW.t[:, ::-1], D1.t[:, ::-1],
                     vv(0, 1), ALU.mult, ALU.add,
                     [NBW.b, D1.b, VV[hp]], [hnxt.b])
                stt(ACC.t[:, :], hnxt.t[:, 0:S], wc(4 + ri), ACC.t[:, :],
                    ALU.mult, ALU.add, [hnxt.b, WCOL.b, ACC.b], [ACC.b])
                hcur, hnxt = hnxt, hcur

            tt(ALU.mult, ACC.t[:, 0:1], WCOL.t[:, 64 + hp:65 + hp], vv(0, 1),
               [WCOL.b, VV[hp]], [ACC.b])
            copy(AA.t[:, hp * S:(hp + 1) * S], ACC.t[:, :], [ACC.b], [AA[hp]])

        es2.close()
        ffn_update(1)
        esL.close()

    def layer2(bi):
        esL = ExitStack()
        AA = anew(esL, "AA", [128, NCH * S], BF16, NCH)
        cur["AA"] = AA

        es2 = ExitStack()
        KK = anew(es2, "KK", [128, NCH * S], BF16, NCH)
        VT = anew(es2, "VT", [128, NKT * 1024], BF16, NKT)
        SQ = anew(es2, "SQ", [128, 512], BF16, 1)
        ED = anew(es2, "ED", [128, 2048], BF16, 4)
        COS = anew(es2, "COS", [128, S], BF16, 1)
        SIN = anew(es2, "SIN", [128, S], BF16, 1)
        PSW = anew(es2, "PSW", [128, 128], BF16, 1)
        SQ2 = anew(es2, "SQ2", [128, 1024], BF16, 1)
        es2a = ExitStack()
        WQ = anew(es2a, "WQ", [128, 8 * 512], BF16, NCH)
        cur["BB"] = KK

        layernorm(SQ.t[:, :], SQ[0])

        for piece in range(4):
            i = piece % 2
            dma("sync", WST.t[:, i * 512:i * 512 + 512],
                cosA_d[:, piece * 512:piece * 512 + 512], writes=[WST[i]])
            copy(COS.t[:, piece * 512:piece * 512 + 512],
                 WST.t[:, i * 512:i * 512 + 512], [WST[i]], [COS.b],
                 eng="gpsimd")
            dma("sync", WST.t[:, i * 512:i * 512 + 512],
                sinA_d[:, piece * 512:piece * 512 + 512], writes=[WST[i]])
            copy(SIN.t[:, piece * 512:piece * 512 + 512],
                 WST.t[:, i * 512:i * 512 + 512], [WST[i]], [SIN.b],
                 eng="gpsimd")
        dma("sync", WST.t[:, 0:128], Pswap_d, writes=[WST[0]])
        copy(PSW.t[:, :], WST.t[:, 0:128], [WST[0]], [PSW.b], eng="gpsimd")

        def q_cb(m, tc, ps):
            act(aa(m, tc * 512, 512), ps.t[:, :], AF.Copy, [ps.b], [AA[m]],
                scale=0.125)

        def k_cb(m, tc, ps):
            act(KK.t[:, m * S + tc * 512: m * S + tc * 512 + 512], ps.t[:, :],
                AF.Copy, [ps.b], [KK[m]])

        proj(Wqkv2, 0, 8, q_cb, wpool=WQ)
        proj(Wqkv2, 8, 8, k_cb, wpool=WQ)

        for vc in range(2):
            for c in range(NCH):
                load_w_cast(WQ.t[:, c * 512: c * 512 + 512], WQ[c],
                            Wqkv2[c * 128:(c + 1) * 128,
                                  2048 + vc * 512: 2048 + vc * 512 + 512])
            for tt_ in range(NKT):
                ps = PS[tt_ % 2]
                for c in range(NCH):
                    mm(ps.t[:, :], xn(c, tt_ * 128, 128),
                       WQ.t[:, c * 512: c * 512 + 512], [XN[c], WQ[c]],
                       ps.b, c == 0, c == NCH - 1)
                act(VT.t[:, tt_ * 1024 + vc * 512: tt_ * 1024 + vc * 512 + 512],
                    ps.t[:, :], AF.Copy, [ps.b], [VT[tt_]])

        es2a.close()
        DENR = anew(es2, "DENR", [1, 2 * S], BF16, 1)
        for hp in range(8):
            for hh in range(2):
                pb = 64 * hh
                NI = NKT * NT

                def den_s(i):
                    kt, qc = divmod(i, NT)
                    ps = PS[i % 3]
                    mm(ps.t[:, :],
                       KK.t[pb:pb + 64, hp * S + kt * 128:
                            hp * S + kt * 128 + 128],
                       AA.t[pb:pb + 64, hp * S + qc * 512:
                            hp * S + qc * 512 + 512],
                       [KK[hp], AA[hp]], ps.b, True, True)
                    eb = ED.bufs[i % 4]
                    et = ED.t[:, (i % 4) * 512:(i % 4) * 512 + 512]
                    act(et, ps.t[:, :], AF.Exp, [ps.b], [eb])

                def den_z(i):
                    kt, qc = divmod(i, NT)
                    eb = ED.bufs[i % 4]
                    et = ED.t[:, (i % 4) * 512:(i % 4) * 512 + 512]
                    ctz, rz = prow(qc)
                    mm(rz, ones_c128.t[:, :], et, [eb, ones_c128.b],
                       ctz.b, kt == 0, kt == NKT - 1)

                for i in range(NI + 2):
                    if i < NI:
                        den_s(i)
                    if i >= 2:
                        den_z(i - 2)
                for qc in range(NT):
                    ctz, rz = prow(qc)
                    act(DENR.t[0:1, hh * S + qc * 512: hh * S + qc * 512 + 512],
                        rz, AF.Copy, [ctz.b], [DENR.b])
            recip(DENR.t[0:1, :], DENR.t[0:1, :], [DENR.b], [DENR.b])

            for tc in range(NT):
                for j_, (TT_, TB) in enumerate(((AA, AA[hp]), (KK, KK[hp]))):
                    psr = PS[5] if (tc * 2 + j_) % 2 == 0 else PS[4]
                    base = hp * S + tc * 512
                    mm(psr.t[:, :], PSW.t[:, :], TT_.t[:, base:base + 512],
                       [PSW.b, TB], psr.b, True, True)
                    tt(ALU.mult, SQ2.t[:, 0:512], TT_.t[:, base:base + 512],
                       COS.t[:, tc * 512:tc * 512 + 512], [TB, COS.b], [SQ2[0]])
                    tt(ALU.mult, SQ2.t[:, 512:1024], psr.t[:, :],
                       SIN.t[:, tc * 512:tc * 512 + 512],
                       [psr.b, SIN.b], [SQ2[0]])
                    tt(ALU.add, TT_.t[:, base:base + 512], SQ2.t[:, 0:512],
                       SQ2.t[:, 512:1024], [SQ2[0], TB], [TB])

            for hh in range(2):
                h = 2 * hp + hh
                pb = 64 * hh
                for qc in range(NT):
                    psc = PS[6] if qc % 2 == 0 else PS[3]
                    nkt_q = 4 * qc + 4

                    def num_s(kt):
                        ps = PS[kt % 3]
                        mm(ps.t[:, :],
                           KK.t[pb:pb + 64, hp * S + kt * 128:
                                hp * S + kt * 128 + 128],
                           AA.t[pb:pb + 64, hp * S + qc * 512:
                                hp * S + qc * 512 + 512],
                           [KK[hp], AA[hp]], ps.b, True, True)
                        eb = ED.bufs[kt % 4]
                        et = ED.t[:, (kt % 4) * 512:(kt % 4) * 512 + 512]
                        act(et, ps.t[:, :], AF.Exp, [ps.b], [eb])
                        if kt >= 4 * qc:
                            dmask = 128 * kt - 512 * qc
                            tt(ALU.mult, et, et,
                               MS2.t[:, 512 - dmask:1024 - dmask],
                               [eb, MS2.b], [eb])

                    def num_av(kt):
                        eb = ED.bufs[kt % 4]
                        et = ED.t[:, (kt % 4) * 512:(kt % 4) * 512 + 512]
                        mm(psc.t[pb:pb + 64, :],
                           VT.t[:, kt * 1024 + h * 64:kt * 1024 + h * 64 + 64],
                           et, [VT[kt], eb], psc.b, kt == 0, kt == nkt_q - 1)

                    for kt in range(nkt_q + 2):
                        if kt < nkt_q:
                            num_s(kt)
                        if kt >= 2:
                            num_av(kt - 2)
                    psr = PS[5] if qc % 2 == 0 else PS[4]
                    mm(psr.t[0:64, :], ones_r128b.t[0:1, 0:64],
                       DENR.t[0:1, hh * S + qc * 512:hh * S + qc * 512 + 512],
                       [DENR.b, ones_r128b.b], psr.b, True, True)
                    act(SQ2.t[0:64, 0:512], psr.t[0:64, :], AF.Copy,
                        [psr.b], [SQ2[0]])
                    tt(ALU.mult,
                       AA.t[pb:pb + 64,
                            hp * S + qc * 512:hp * S + qc * 512 + 512],
                       psc.t[pb:pb + 64, :], SQ2.t[0:64, 0:512],
                       [psc.b, SQ2[0], AA[hp]], [AA[hp]])

        es2.close()

        SSn = anew(esL, "SS", [128, NCH * S], BF16, NCH)
        WQb = anew(esL, "WQb", [128, 8 * 512], BF16, NCH)
        cur["SS"] = SSn
        bo = smalls["bout2"][0]
        bob = smalls["bout2"][1]

        def s_cb(m, tc, ps):
            act(SSn.t[:, m * S + tc * 512: m * S + tc * 512 + 512], ps.t[:, :],
                AF.Copy, [ps.b, bob], [SSn[m]], bias=bo[:, m:m + 1])

        proj(Wout2, 0, 8, s_cb, wpool=WQb, src_fn=aa, src_bufs=AA.bufs)
        ffn2(bi)
        esL.close()

    def ffn2(bi):
        es2 = ExitStack()
        TG = 512
        HH = anew(es2, "HH", [128, 24 * TG], BF16, 24)
        WG = anew(es2, "WG", [128, 8 * 512], BF16, NCH)
        WU = anew(es2, "WU", [128, 8 * 512], BF16, NCH)
        SQ = anew(es2, "SQ", [128, 1024], BF16, 1)
        SS = cur["SS"]
        for tg in range(S // TG):
            t0 = tg * TG
            ntc = TG // 512
            for g0 in range(0, 24, 4):
                for c in range(NCH):
                    load_w_cast(WG.t[:, c * 512: c * 512 + 512], WG[c],
                                wgT2[c * 128:(c + 1) * 128,
                                     g0 * 128:(g0 + 4) * 128])
                    load_w_cast(WU.t[:, c * 512: c * 512 + 512], WU[c],
                                wuT2[c * 128:(c + 1) * 128,
                                     g0 * 128:(g0 + 4) * 128])
                for ml in range(4):
                    m = g0 + ml
                    pg = PS[ml % 2]
                    pu = PS[2 + ml % 2]
                    for c in range(NCH):
                        mm(pg.t[:, :],
                           WG.t[:, c * 512 + ml * 128: c * 512 + (ml + 1) * 128],
                           xn(c, t0, 512), [WG[c], XN[c]],
                           pg.b, c == 0, c == NCH - 1)
                    act(HH.t[:, m * TG: m * TG + 512],
                        pg.t[:, :], AF.Silu, [pg.b], [HH[m]],
                        scale=1.0 / 127.0)
                    for c in range(NCH):
                        mm(pu.t[:, :],
                           WU.t[:, c * 512 + ml * 128: c * 512 + (ml + 1) * 128],
                           xn(c, t0, 512), [WU[c], XN[c]],
                           pu.b, c == 0, c == NCH - 1)
                    i2 = ml % 2
                    act(SQ.t[:, i2 * 512:i2 * 512 + 512], pu.t[:, :], AF.Copy,
                        [pu.b], [SQ[0]], scale=1.0 / 127.0)
                    tt(ALU.mult,
                       HH.t[:, m * TG: m * TG + 512],
                       HH.t[:, m * TG: m * TG + 512],
                       SQ.t[:, i2 * 512:i2 * 512 + 512], [SQ[0], HH[m]], [HH[m]])
            for m in range(NCH):
                for g0 in range(0, 24, 8):
                    for cc in range(8):
                        load_w_cast(WG.t[:, cc * 512: cc * 512 + 128],
                                    WG[cc],
                                    wdT2[(g0 + cc) * 128:(g0 + cc + 1) * 128,
                                         m * 128:(m + 1) * 128])
                    for cc in range(8):
                        ci = g0 + cc
                        mm(PS[m % 3].t[:, :],
                           WG.t[:, cc * 512: cc * 512 + 128],
                           HH.t[:, ci * TG: ci * TG + 512],
                           [WG[cc], HH[ci]],
                           PS[m % 3].b, ci == 0, ci == 23)
                ps = PS[m % 3]
                act(SQ.t[:, 512:1024], ps.t[:, :], AF.Silu, [ps.b], [SQ[0]],
                    scale=1.0 / 127.0)
                tt(ALU.mult, SQ.t[:, 0:512], SQ.t[:, 512:1024],
                   SS.t[:, m * S + t0: m * S + t0 + 512],
                   [SQ[0], SS[m]], [SQ[0]])
                tt(ALU.add, xr(m, t0, 512), xr(m, t0, 512),
                   SQ.t[:, 0:512], [SQ[0], XR[m]], [XR[m]])
        es2.close()

    def layer3(bi):
        esL = ExitStack()
        AA = anew(esL, "AA", [128, NCH * S], BF16, NCH)
        cur["AA"] = AA

        es2 = ExitStack()
        KK = anew(es2, "KK", [128, NCH * S], BF16, NCH)
        VT = anew(es2, "VT", [128, NKT * 1040], BF16, NKT)
        SQ = anew(es2, "SQ", [128, 1024], BF16, 2)
        es2a = ExitStack()
        WQ = anew(es2a, "WQ", [128, 8 * 512], BF16, NCH)
        BVR = anew(es2a, "BVR", [1, 1024], BF16, 1)
        cur["BB"] = KK

        layernorm(SQ.t[:, 0:512], SQ[0])

        bq = smalls["bq3"][0]
        bqb = smalls["bq3"][1]
        bk = smalls["bk3"][0]
        bkb = smalls["bk3"][1]
        ts(SM2.t[:, 40:48], bk, 0.125, [bkb], [SM2[2]])

        def q_cb(m, tc, ps):
            act(aa(m, tc * 512, 512), ps.t[:, :], AF.Copy, [ps.b, bqb],
                [AA[m]], bias=bq[:, m:m + 1])

        def k_cb(m, tc, ps):
            act(KK.t[:, m * S + tc * 512: m * S + tc * 512 + 512], ps.t[:, :],
                AF.Copy, [ps.b, SM2[2]], [KK[m]], bias=SM2.t[:, 40 + m:41 + m],
                scale=0.125)

        proj(wqT3, 0, 8, q_cb, wpool=WQ)
        proj(wkT3, 0, 8, k_cb, wpool=WQ)

        dma("sync", WST.t[0:1, 0:1024], bv3row, writes=[WST[0], WST[1]])
        copy(BVR.t[0:1, :], WST.t[0:1, 0:1024], [WST[0]], [BVR.b], eng="gpsimd")
        for vc in range(2):
            for c in range(NCH):
                load_w_cast(WQ.t[:, c * 512: c * 512 + 512], WQ[c],
                            wvT3[c * 128:(c + 1) * 128,
                                 vc * 512: vc * 512 + 512])
            for tt_ in range(NKT):
                ps = PS[tt_ % 2]
                for c in range(NCH):
                    mm(ps.t[:, :], xn(c, tt_ * 128, 128),
                       WQ.t[:, c * 512: c * 512 + 512], [XN[c], WQ[c]],
                       ps.b, c == 0, False)
                mm(ps.t[:, :], ones_r128b.t[0:1, :],
                   BVR.t[0:1, vc * 512: vc * 512 + 512], [BVR.b, ones_r128b.b],
                   ps.b, False, True)
                hbase = vc * 8 * 65
                dst = bass.AP(VT.t, tt_ * 1040 + hbase,
                              [[NKT * 1040, 128], [65, 8], [1, 64]])
                act(dst, ps.t[:, :], AF.Copy, [ps.b], [VT[tt_]])
        for tt_ in range(NKT):
            ones_ap = bass.AP(VT.t, tt_ * 1040 + 64,
                              [[NKT * 1040, 128], [65, 16], [1, 1]])
            memset(ones_ap, 1.0, [VT[tt_]])
        es2a.close()

        RELB = CT(None, 1)

        for h in range(H):
            hp, hh = h // 2, h % 2
            pb = 64 * hh
            es2b = ExitStack()
            TTB = anew(es2b, "TTB", [128, 4096], BF16, 1)
            BR = anew(es2b, "BR", [128, 2560], BF16, 1)

            for piece in range(8):
                i = piece % 2
                n = 512 if piece < 7 else 4095 - 7 * 512
                dma("sync", WST.t[0:64, i * 512: i * 512 + n],
                    T2[h, :, piece * 512: piece * 512 + n], writes=[WST[i]])
                copy(TTB.t[0:64, piece * 512: piece * 512 + n],
                     WST.t[0:64, i * 512: i * 512 + n], [WST[i]], [TTB.b],
                     eng="gpsimd")
                dma("gpsimd", TTB.t[64:128, piece * 512: piece * 512 + n],
                    WST.t[0:64, i * 512: i * 512 + n], reads=[WST[i]],
                    writes=[TTB.b])
            memset(TTB.t[:, 4095:4096], 0.0, [TTB.b])

            for qt in range(NKT):
                a0 = 1920 - 128 * qt
                A0 = (a0 // 512) * 512
                for ch in range(5):
                    ps = PS[ch % 2]
                    i0 = A0 + ch * 512
                    mm(ps.t[:, :],
                       AA.t[pb:pb + 64, hp * S + qt * 128: hp * S + qt * 128 + 128],
                       TTB.t[pb:pb + 64, i0:i0 + 512], [AA[hp], TTB.b],
                       ps.b, True, True)
                    act(BR.t[:, ch * 512:(ch + 1) * 512], ps.t[:, :], AF.Exp,
                        [ps.b], [BR.b])
                dma("sync",
                    bass.AP(relscr, qt * 128 * 2560, [[2560, 128], [1, 2560]]),
                    BR.t[:, :], reads=[BR.b], writes=[RELB.b])
            es2b.close()
            es2c = ExitStack()
            XPR = anew(es2c, "XPR", [128, S], BF16, 1)
            E4 = anew(es2c, "E4", [128, 4 * S], BF16, 4)
            ET = anew(es2c, "ET", [128, 1024], BF16, 2)
            rzr_t = srow("rz", 0, 512)

            for qc in range(NT):
                psc = PS[6] if qc % 2 == 0 else PS[3]
                for qi in range(4):
                    qt = 4 * qc + qi
                    a0 = 1920 - 128 * qt
                    A0 = (a0 // 512) * 512
                    coff = qt * 128 * 2560 + 2047 - qt * 128 - A0
                    dma("sync", XPR.t[:, :],
                        bass.AP(relscr, coff, [[2559, 128], [1, 2048]]),
                        reads=[RELB.b], writes=[XPR.b])
                    for kc in range(NT):
                        ps = PS[(0, 1, 2, 4)[kc]]
                        mm(ps.t[:, :],
                           AA.t[pb:pb + 64,
                                hp * S + qt * 128: hp * S + qt * 128 + 128],
                           KK.t[pb:pb + 64,
                                hp * S + kc * 512: hp * S + kc * 512 + 512],
                           [AA[hp], KK[hp]], ps.b, True, True)
                        sqb = SQ.bufs[kc % 2]
                        sqt = SQ.t[:, (kc % 2) * 512:(kc % 2) * 512 + 512]
                        act(sqt, ps.t[:, :], AF.Exp, [ps.b], [sqb])
                        tt(ALU.mult,
                           E4.t[:, qi * S + kc * 512: qi * S + kc * 512 + 512],
                           sqt, XPR.t[:, kc * 512: kc * 512 + 512],
                           [sqb, XPR.b], [E4[qi]])
                for kt in range(NKT):
                    eb = ET.bufs[kt % 2]
                    et = ET.t[:, (kt % 2) * 512:(kt % 2) * 512 + 512]
                    for qi in range(4):
                        transp(PT.t[:, qi * 128:(qi + 1) * 128],
                               E4.t[:, qi * S + kt * 128: qi * S + kt * 128 + 128],
                               [E4[qi]], [PT.b])
                    act(et, PT.t[:, :], AF.Copy, [PT.b], [eb])
                    mm(psc.t[0:65, :],
                       VT.t[:, kt * 1040 + h * 65: kt * 1040 + h * 65 + 65],
                       et, [VT[kt], eb], psc.b, kt == 0, kt == NKT - 1)
                act(rzr_t, psc.t[64:65, :], AF.Copy, [psc.b], [STAT.b])
                recip(rzr_t, rzr_t, [STAT.b], [STAT.b])
                mm(PS[5].t[0:64, :], ones_r128b.t[0:1, 0:64], rzr_t,
                   [STAT.b, ones_r128b.b], PS[5].b, True, True)
                act(SQ.t[0:64, 0:512], PS[5].t[0:64, :], AF.Copy,
                    [PS[5].b], [SQ[0]])
                tt(ALU.mult,
                   AA.t[pb:pb + 64, hp * S + qc * 512: hp * S + qc * 512 + 512],
                   psc.t[0:64, :], SQ.t[0:64, 0:512],
                   [psc.b, SQ[0], AA[hp]], [AA[hp]])
            es2c.close()

        es2.close()

        esW = ExitStack()
        WQb = anew(esW, "WQb", [128, 8 * 512], BF16, NCH)
        SEV = anew(esW, "SEV", [128, 1024], BF16, 2)
        SSCR = CT(None, 1)
        cur["S3"] = SSCR
        bo = smalls["bo3"][0]
        bob = smalls["bo3"][1]

        def s_cb(m, tc, ps):
            i = (m * NT + tc) % 2
            act(SEV.t[:, i * 512:(i + 1) * 512], ps.t[:, :],
                AF.Copy, [ps.b, bob], [SEV.bufs[i]], bias=bo[:, m:m + 1])
            dma("sync", s3scr[m * 128:(m + 1) * 128, tc * 512: tc * 512 + 512],
                SEV.t[:, i * 512:(i + 1) * 512], reads=[SEV.bufs[i]],
                writes=[SSCR.b])

        proj(woT3, 0, 8, s_cb, wpool=WQb, src_fn=aa, src_bufs=AA.bufs)
        esW.close()
        esL.close()
        ffn3(bi)

    def quant_weight_to_scratch(w_dram, n_k, n_m, scratch_dram, es_unused):
        es = ExitStack()
        QW = anew(es, "QW", [128, 8 * 512], F32, 8)
        QS = anew(es, "QS", [128, 1024], F32, 1)
        QB = anew(es, "QB", [128, 512], BF16, 1)
        QI = anew(es, "QI", [128, 512], I32, 1)
        QC = anew(es, "QC", [128, 8], F32, 1)
        QR = anew(es, "QR", [1, 8], F32, 1)
        npieces = n_m * 128 // 512
        idx = 0
        tot_pieces = n_k * npieces
        for c in range(n_k):
            for pc in range(npieces):
                i = idx % 8
                dma("gpsimd", QW.t[:, i * 512: i * 512 + 512],
                    w_dram[c * 128:(c + 1) * 128, pc * 512: pc * 512 + 512],
                    writes=[QW.bufs[i]])
                stt(QS.t[:, 0:512], QW.t[:, i * 512: i * 512 + 512], -1.0,
                    QW.t[:, i * 512: i * 512 + 512], ALU.mult, ALU.max,
                    [QW.bufs[i]], [QS.b])
                r = PS[3].t[0:1, 0:512]
                mm(r, ones_c128f.t[:, :], QS.t[:, 0:512],
                   [QS.b, ones_c128f.b], PS[3].b, idx == 0,
                   idx == tot_pieces - 1)
                idx += 1
        ct, r = PS[3], PS[3].t[0:1, 0:512]
        p.op("vector",
             lambda h_, o=QR.t[0:1, 0:1], i=r:
             h_.reduce_sum(o, i, axis=mybir.AxisListType.X),
             reads=[ct.b], writes=[QR.b])
        tot = float(n_k * 128 * n_m * 128)
        ts(QR.t[0:1, 1:2], QR.t[0:1, 0:1], 1.0 / tot, [QR.b], [QR.b],
           s2=1e-5, op1=ALU.max)
        recip(QR.t[0:1, 2:3], QR.t[0:1, 1:2], [QR.b], [QR.b])
        mm(PS[5].t[:, 0:2], ones_r128f.t[0:1, :],
           QR.t[0:1, 1:3], [QR.b, ones_r128f.b], PS[5].b, True, True)
        act(QC.t[:, 0:1], PS[5].t[:, 1:2], AF.Copy, [PS[5].b], [QC.b])
        act(QC.t[:, 1:2], PS[5].t[:, 0:1], AF.Copy, [PS[5].b], [QC.b])
        idx = 0
        for c in range(n_k):
            for pc in range(npieces):
                i = idx % 8
                idx += 1
                dma("gpsimd", QW.t[:, i * 512: i * 512 + 512],
                    w_dram[c * 128:(c + 1) * 128, pc * 512: pc * 512 + 512],
                    writes=[QW.bufs[i]])
                ts(QS.t[:, 0:512], QW.t[:, i * 512: i * 512 + 512],
                   QC.t[:, 0:1], [QW.bufs[i], QC.b], [QS.b])
                copy(QI.t[:, :], QS.t[:, 0:512], [QS.b], [QI.b])
                ts(QI.t[:, :], QI.t[:, :], 1, [QI.b], [QI.b],
                   op0=ALU.min, s2=-1, op1=ALU.max)
                ts(QB.t[:, :], QI.t[:, :],
                   QC.t[:, 1:2], [QI.b, QC.b], [QB.b])
                dma("gpsimd",
                    scratch_dram[c * 128:(c + 1) * 128, pc * 512: pc * 512 + 512],
                    QB.t[:, :], reads=[QB.b], writes=[RELQ.b])
        es.close()

    def ffn3(bi):
        es2 = ExitStack()
        SSCR = cur["S3"]
        SP = anew(es2, "SP", [128, 1024], BF16, 2)
        AM = anew(es2, "AM", [128, S], BF16, 1)
        MXR = anew(es2, "MXR", [1, 2 * S], BF16, 1)
        SQ = anew(es2, "SQ", [128, 1024], BF16, 1)
        QI = anew(es2, "QI", [128, 512], I32, 1)
        QF = anew(es2, "QF", [128, 512], F32, 1)
        MXT = anew(es2, "MXT", [128, 16], BF16, 1)

        def act_quant(src_fn, src_bufs, nch, width, dst_fn, dst_bufs):
            stt(AM.t[:, 0:width], src_fn(0, 0, width), -1.0,
                src_fn(0, 0, width), ALU.mult, ALU.max, [src_bufs[0]], [AM.b])
            for c in range(1, nch):
                for pp in range(0, width, 512):
                    n = min(512, width - pp)
                    stt(SQ.t[:, 0:n], src_fn(c, pp, n), -1.0, src_fn(c, pp, n),
                        ALU.mult, ALU.max, [src_bufs[c]], [SQ[0]])
                    tt(ALU.max, AM.t[:, pp:pp + n], AM.t[:, pp:pp + n],
                       SQ.t[:, 0:n], [SQ[0], AM.b], [AM.b])
            ntt = width // 128
            for b_ in range(ntt):
                transp(PT.t[:, 0:128], AM.t[:, b_ * 128:(b_ + 1) * 128],
                       [AM.b], [PT.b])
                act(SQ.t[:, 0:128], PT.t[:, 0:128], AF.Copy, [PT.b], [SQ[0]])
                p.op("vector",
                     lambda h_, o=MXT.t[:, b_:b_ + 1], i=SQ.t[:, 0:128]:
                     h_.reduce_max(o, i, axis=mybir.AxisListType.X),
                     reads=[SQ[0]], writes=[MXT.b])
            transp(PT.t[0:16, 0:128], MXT.t[:, 0:16], [MXT.b], [PT.b])
            copy(SQ.t[0:16, 512:640], PT.t[0:16, 0:128], [PT.b], [SQ[0]])
            for b_ in range(ntt):
                dma("gpsimd", MXR.t[0:1, b_ * 128:(b_ + 1) * 128],
                    SQ.t[b_:b_ + 1, 512:640], reads=[SQ[0]], writes=[MXR.b])
            ts(MXR.t[0:1, 0:width], MXR.t[0:1, 0:width], 1e-5,
               [MXR.b], [MXR.b], op0=ALU.max)
            ts(MXR.t[0:1, S:S + width], MXR.t[0:1, 0:width], 1.0 / 127.0,
               [MXR.b], [MXR.b])
            recip(MXR.t[0:1, 0:width], MXR.t[0:1, 0:width], [MXR.b], [MXR.b])
            ts(MXR.t[0:1, 0:width], MXR.t[0:1, 0:width], 127.0,
               [MXR.b], [MXR.b])
            ntc = (width + 511) // 512
            for tc in range(ntc):
                n = min(512, width - tc * 512)
                mm(PS[5].t[:, 0:n], ones_r128b.t[0:1, :],
                   MXR.t[0:1, tc * 512: tc * 512 + n], [MXR.b, ones_r128b.b],
                   PS[5].b, True, True)
                mm(PS[4].t[:, 0:n], ones_r128b.t[0:1, :],
                   MXR.t[0:1, S + tc * 512: S + tc * 512 + n],
                   [MXR.b, ones_r128b.b], PS[4].b, True, True)
                for c in range(nch):
                    tt(ALU.mult, QF.t[:, 0:n], src_fn(c, tc * 512, n),
                       PS[5].t[:, 0:n], [src_bufs[c], PS[5].b], [QF.b])
                    copy(QI.t[:, 0:n], QF.t[:, 0:n], [QF.b], [QI.b])
                    ts(QI.t[:, 0:n], QI.t[:, 0:n], 127, [QI.b], [QI.b],
                       op0=ALU.min, s2=-128, op1=ALU.max)
                    tt(ALU.mult, dst_fn(c, tc * 512, n), QI.t[:, 0:n],
                       PS[4].t[:, 0:n], [QI.b, PS[4].b], [dst_bufs[c]])

        act_quant(xn, XN.bufs, NCH, S, xn, XN.bufs)

        TG = 512
        HH = anew(es2, "HH", [128, 32 * TG], BF16, 32)
        WG = anew(es2, "WG", [128, 8 * 512], BF16, NCH)
        WU = anew(es2, "WU", [128, 8 * 512], BF16, NCH)

        def hh_fn(c, c0, n):
            return HH.t[:, c * TG + c0: c * TG + c0 + n]

        for tg in range(S // TG):
            t0 = tg * TG
            for g0 in range(0, 32, 4):
                for c in range(NCH):
                    load_w_cast(WG.t[:, c * 512: c * 512 + 512], WG[c],
                                wg3q[c * 128:(c + 1) * 128,
                                     g0 * 128:(g0 + 4) * 128])
                    load_w_cast(WU.t[:, c * 512: c * 512 + 512], WU[c],
                                wu3q[c * 128:(c + 1) * 128,
                                     g0 * 128:(g0 + 4) * 128])
                for ml in range(4):
                    m = g0 + ml
                    pg = PS[ml % 2]
                    pu = PS[2 + ml % 2]
                    for c in range(NCH):
                        mm(pg.t[:, :],
                           WG.t[:, c * 512 + ml * 128: c * 512 + (ml + 1) * 128],
                           xn(c, t0, 512), [WG[c], XN[c]],
                           pg.b, c == 0, c == NCH - 1)
                    act(HH.t[:, m * TG: m * TG + 512], pg.t[:, :], AF.Silu,
                        [pg.b], [HH[m]])
                    for c in range(NCH):
                        mm(pu.t[:, :],
                           WU.t[:, c * 512 + ml * 128: c * 512 + (ml + 1) * 128],
                           xn(c, t0, 512), [XN[c], WU[c]],
                           pu.b, c == 0, c == NCH - 1)
                    i2 = ml % 2
                    act(SQ.t[:, i2 * 512:i2 * 512 + 512], pu.t[:, :], AF.Copy,
                        [pu.b], [SQ[0]])
                    tt(ALU.mult, HH.t[:, m * TG: m * TG + 512],
                       HH.t[:, m * TG: m * TG + 512],
                       SQ.t[:, i2 * 512:i2 * 512 + 512], [SQ[0], HH[m]], [HH[m]])
            act_quant(hh_fn, HH.bufs, 32, TG, hh_fn, HH.bufs)
            for m in range(NCH):
                for g0 in range(0, 32, 8):
                    for cc in range(8):
                        load_w_cast(WG.t[:, cc * 512: cc * 512 + 128], WG[cc],
                                    wd3q[(g0 + cc) * 128:(g0 + cc + 1) * 128,
                                         m * 128:(m + 1) * 128])
                    for cc in range(8):
                        ci = g0 + cc
                        mm(PS[m % 3].t[:, :], WG.t[:, cc * 512: cc * 512 + 128],
                           hh_fn(ci, 0, 512), [WG[cc], HH[ci]],
                           PS[m % 3].b, ci == 0, ci == 31)
                act(SQ.t[:, 512:1024], PS[m % 3].t[:, :], AF.Silu, [PS[m % 3].b],
                    [SQ[0]])
                i = m % 2
                dma("sync", SP.t[:, i * 512:(i + 1) * 512],
                    s3scr[m * 128:(m + 1) * 128, t0: t0 + 512],
                    reads=[SSCR.b], writes=[SP.bufs[i]])
                tt(ALU.mult, SQ.t[:, 0:512], SQ.t[:, 512:1024],
                   SP.t[:, i * 512:(i + 1) * 512], [SQ[0], SP.bufs[i]],
                   [SQ[0]])
                tt(ALU.add, xr(m, t0, 512), xr(m, t0, 512), SQ.t[:, 0:512],
                   [SQ[0], XR[m]], [XR[m]])
        es2.close()

    def ffn_update(li):
        es2 = ExitStack()
        AA = anew(es2, "AA", [128, NCH * S], BF16, NCH)
        BB = anew(es2, "BB", [128, NCH * S], BF16, NCH)
        SQ = anew(es2, "SQ", [128, 1024], BF16, 1)
        WQ = anew(es2, "WQ", [128, 8 * 512], BF16, NCH)
        WQ2 = anew(es2, "WQ2", [128, 8 * 512], BF16, NCH)
        cur["AA"], cur["BB"] = AA, BB
        SS = cur["SS"]
        mur = smalls[f"mur{li}"][0]
        murb = smalls[f"mur{li}"][1]
        muk = smalls[f"muk{li}"][0]
        mukb = smalls[f"muk{li}"][1]
        ts(SM2.t[:, 16:24], mur, -1.0, [murb], [SM2[1]], s2=1.0, op1=ALU.add)
        ts(SM2.t[:, 24:32], muk, -1.0, [mukb], [SM2[1]], s2=1.0, op1=ALU.add)

        def r_cb(m, tc, ps):
            act(aa(m, tc * 512, 512), ps.t[:, :], AF.Sigmoid, [ps.b], [AA[m]])

        def k_cb(m, tc, ps):
            act(SQ.t[:, 0:512], ps.t[:, :], AF.Relu, [ps.b], [SQ[0]])
            tt(ALU.mult, bbc(m, tc * 512, 512), SQ.t[:, 0:512], SQ.t[:, 0:512],
               [SQ[0]], [BB[m]])

        proj(ffnW[(li, "r")], 0, 8, r_cb, wpool=WQ,
             scale_col=lambda c: (mur[:, c:c + 1], murb),
             shifted_pair=(ffnW[(li, "r")],
                           lambda c: (SM2.t[:, 16 + c:17 + c], SM2[1])),
             wpool2=WQ2)
        proj(ffnW[(li, "k")], 0, 8, k_cb, wpool=WQ,
             scale_col=lambda c: (muk[:, c:c + 1], mukb),
             shifted_pair=(ffnW[(li, "k")],
                           lambda c: (SM2.t[:, 24 + c:25 + c], SM2[1])),
             wpool2=WQ2)

        def o_cb(m, tc, ps):
            tt(ALU.mult, SQ.t[:, 0:512], ps.t[:, :], aa(m, tc * 512, 512),
               [ps.b, AA[m]], [SQ[0]])
            act(SQ.t[:, 512:1024], SQ.t[:, 0:512], AF.Silu, [SQ[0]], [SQ[0]])
            tt(ALU.mult, SQ.t[:, 0:512], SQ.t[:, 512:1024],
               sschunk(m, tc * 512, 512), [SQ[0], SS[m]], [SQ[0]])
            tt(ALU.add, xr(m, tc * 512, 512), xr(m, tc * 512, 512),
               SQ.t[:, 0:512], [SQ[0], XR[m]], [XR[m]])

        def k2src(c, t0, n):
            return bbc(c, t0, n)

        proj(ffnW[(li, "v")], 0, 8, o_cb, src_fn=k2src, src_bufs=BB.bufs,
             wpool=WQ)
        es2.close()

    def batch(bi, layers):
        for c in range(NCH):
            for piece in range(4):
                i = piece % 2
                dma("sync", WST.t[:, i * 512: i * 512 + 512],
                    x_in[bi, c * 128:(c + 1) * 128, piece * 512:piece * 512 + 512],
                    writes=[WST[i]])
                copy(xr(c, piece * 512, 512), WST.t[:, i * 512: i * 512 + 512],
                     [WST[i]], [XR[c]], eng="gpsimd")
        if 0 in layers:
            layer0(bi)
        if bi == 0:
            quant_weight_to_scratch(wg3T, 8, 32, wg3q, None)
        if 1 in layers:
            layer1(bi)
        if bi == 0:
            quant_weight_to_scratch(wu3T, 8, 32, wu3q, None)
        if 2 in layers:
            layer2(bi)
        if bi == 0:
            quant_weight_to_scratch(wd3T, 32, 8, wd3q, None)
        if 3 in layers:
            layer3(bi)
        for c in range(NCH):
            for piece in range(4):
                i = piece % 2
                copy(WST.t[:, i * 512: i * 512 + 512], xr(c, piece * 512, 512),
                     [XR[c]], [WST[i]], eng="gpsimd")
                dma("sync", y_out[bi, c * 128:(c + 1) * 128,
                                  piece * 512:piece * 512 + 512],
                    WST.t[:, i * 512: i * 512 + 512], reads=[WST[i]])

    import os as _os
    KPHASE = int(_os.environ.get("KPHASE", "99"))
    build_consts()
    load_small("bqkv0", bqkv0, 0)
    load_small("vconv", vconv, 1)
    load_small("mur0", ffnW[(0, "mur")], 2)
    load_small("muk0", ffnW[(0, "muk")], 3)
    load_small("mur1", ffnW[(1, "mur")], 4)
    load_small("bk1", bk1, 6)
    load_small("bv1", bv1, 7)
    load_small("bout2", bout2, 1, pool=SM2)
    load_small("bq3", bq3, 2, pool=SM2)
    load_small("bk3", bk3, 3, pool=SM2)
    load_small("bo3", bo3, 4, pool=SM2)
    load_small("muk1", ffnW[(1, "muk")], 5)

    layers = set(range(n_layers))
    nb = int(_os.environ.get("KBATCH", "2"))
    for bi in range(nb):
        batch(bi, layers)

    print("[build] eng ops:", p.eng_n, "dma:", p.dma_n)
    p.emit(nc, es)
    es.close()
    return nc


def _prep_inputs(kw, core):
    b0 = 2 * core
    x = kw["L0_attn_x"][b0:b0 + 2]
    d = {}
    d["x_cm"] = np.ascontiguousarray(x.transpose(0, 2, 1))
    d["W_qkv0"] = np.ascontiguousarray(kw["L0_attn_W_qkv"])
    d["bqkv0"] = np.ascontiguousarray(kw["L0_attn_b_qkv"].reshape(24, 128).T)
    d["vconv"] = np.ascontiguousarray(
        kw["L0_attn_W_vconv"].reshape(8, 128, 3).transpose(1, 0, 2).reshape(128, 24))
    for li, pre in ((0, "L0_ffn"), (1, "L1_ffn")):
        for nm in ("r", "k", "v"):
            d[f"ffn{li}_W{nm}T"] = np.ascontiguousarray(kw[f"{pre}_W_{nm}"].T)
        d[f"ffn{li}_mur"] = np.ascontiguousarray(kw[f"{pre}_mu_r"].reshape(8, 128).T)
        d[f"ffn{li}_muk"] = np.ascontiguousarray(kw[f"{pre}_mu_k"].reshape(8, 128).T)
    d["Wk1"] = np.ascontiguousarray(kw["L1_attn_Wk"])
    d["Wv1"] = np.ascontiguousarray(kw["L1_attn_Wv"])
    d["bk1"] = np.ascontiguousarray(kw["L1_attn_bk"].reshape(8, 128).T)
    d["bv1"] = np.ascontiguousarray(kw["L1_attn_bv"].reshape(8, 128).T)
    rh = kw["L1_attn_ReadingHead"]
    rmat = np.zeros((1024, 16), np.float32)
    for h in range(16):
        rmat[64 * h:64 * h + 64, h] = rh[h]
    d["RHmat"] = rmat
    d["bidir"] = np.ascontiguousarray(kw["L1_attn_bidirection_weight"].reshape(1, 128))
    sel = np.zeros((16, 1024), np.float32)
    for hp in range(8):
        for h2 in range(2):
            sel[2 * hp + h2, 128 * hp + 64 * h2:128 * hp + 64 * h2 + 64] = 1.0
    d["selHP"] = sel
    wq2 = kw["L2_attn_Wqkv"].copy()
    perm = np.concatenate([np.arange(0, 64, 2), np.arange(1, 64, 2)])
    colperm = np.arange(3072)
    for sec in range(2):
        for h in range(16):
            base = sec * 1024 + h * 64
            colperm[base:base + 64] = base + perm
    d["Wqkv2"] = np.ascontiguousarray(wq2[:, colperm])
    d["Wout2"] = np.ascontiguousarray(kw["L2_attn_Wout"])
    d["bout2"] = np.ascontiguousarray(kw["L2_attn_bout"].reshape(8, 128).T)
    i = np.arange(32, dtype=np.float64)
    theta = np.power(10000.0, -2.0 * (i - 1.0) / 64).astype(np.float64)
    ang = np.arange(2048, dtype=np.float64)[:, None] * theta[None, :]
    cos = np.cos(ang).astype(np.float32)
    sin = np.sin(ang).astype(np.float32)
    cosA = np.zeros((128, 2048), np.float32)
    sinA = np.zeros((128, 2048), np.float32)
    for blk in range(4):
        r0 = 32 * blk
        cosA[r0:r0 + 32] = cos.T
        sinA[r0:r0 + 32] = sin.T * (1.0 if (blk % 2) else -1.0)
    d["cosA"] = cosA
    d["sinA"] = sinA
    P = np.zeros((128, 128), np.float32)
    for m in range(128):
        blk = m // 32
        src = m + 32 if blk % 2 == 0 else m - 32
        P[src, m] = 1.0
    d["Pswap"] = P
    d["wgT2"] = np.ascontiguousarray(kw["L2_ffn_w_gate"].T)
    d["wuT2"] = np.ascontiguousarray(kw["L2_ffn_w_up"].T)
    d["wdT2"] = np.ascontiguousarray(kw["L2_ffn_w_down"].T)
    d["wqT3"] = np.ascontiguousarray(kw["L3_attn_wq"].T)
    d["wkT3"] = np.ascontiguousarray(kw["L3_attn_wk"].T)
    d["wvT3"] = np.ascontiguousarray(kw["L3_attn_wv"].T)
    d["woT3"] = np.ascontiguousarray(kw["L3_attn_wo"].T)
    d["bq3"] = np.ascontiguousarray(kw["L3_attn_bq"].reshape(8, 128).T)
    d["bk3"] = np.ascontiguousarray(kw["L3_attn_bk"].reshape(8, 128).T)
    d["bo3"] = np.ascontiguousarray(kw["L3_attn_bo"].reshape(8, 128).T)
    d["bv3row"] = np.ascontiguousarray(kw["L3_attn_bv"].reshape(1, 1024))
    d["T2"] = np.ascontiguousarray(
        kw["L3_attn_rpe_table"][:, :4095][:, ::-1, :].transpose(0, 2, 1))
    d["wg3T"] = np.ascontiguousarray(kw["L3_ffn_w_gate"].T)
    d["wu3T"] = np.ascontiguousarray(kw["L3_ffn_w_up"].T)
    d["wd3T"] = np.ascontiguousarray(kw["L3_ffn_w_down"].T)
    return d


_nc_cache = {}


def kernel(**kw):
    import os as _os
    n_layers = kw.pop("_n_layers", 4)
    ncores = int(_os.environ.get("KCORES", NCORES))
    if n_layers not in _nc_cache:
        _nc_cache[n_layers] = build_program(n_layers)
    nc = _nc_cache[n_layers]
    in_maps = [_prep_inputs(kw, core) for core in range(ncores)]
    res = run_bass_kernel_spmd(nc, in_maps, core_ids=list(range(ncores)))
    outs = []
    for core in range(ncores):
        y = res.results[core]["y_cm"]
        outs.append(np.ascontiguousarray(y.transpose(0, 2, 1)))
    out = np.concatenate(outs, axis=0).astype(np.float32)
    if ncores < NCORES:
        out = np.concatenate([out] + [out[:2]] * (2 * (NCORES - ncores)))[:16]
    return out


# version 13 (final)
# speedup vs baseline: 1.0084x; 1.0084x over previous
import numpy as np
from contextlib import ExitStack

import concourse.bass as bass
import concourse.mybir as mybir
from concourse.bass_utils import run_bass_kernel_spmd

F32 = mybir.dt.float32
BF16 = mybir.dt.bfloat16
I32 = mybir.dt.int32
AF = mybir.ActivationFunctionType
ALU = mybir.AluOpType

NCORES = 8
S = 2048
D = 1024
NCH = 8
NT = 4
NKT = 16
H = 16
HD = 64
XNP = 8
XNW = S + XNP

ENGINES = ("tensor", "vector", "scalar", "gpsimd", "sync")


class Buf:
    __slots__ = ("w", "r")

    def __init__(self, seed=()):
        self.w = list(seed)
        self.r = []


DMA_LANES = 4


class Prog:
    def __init__(self):
        self.items = {e: [] for e in ENGINES}
        self.eng_n = {e: 0 for e in ENGINES}
        self.dma_n = {e: 0 for e in ENGINES}
        self.waited = {e: {} for e in ENGINES}

    def op(self, eng, emit, reads=(), writes=(), dma=False):
        deps = []
        for b in reads:
            deps += b.w
        for b in writes:
            deps += b.w
            deps += b.r
        if dma:
            j = self.dma_n[eng]
            lane = j % DMA_LANES
            if j >= DMA_LANES:
                deps = deps + [(("d", eng, lane), 16 * (j // DMA_LANES))]
            self.dma_n[eng] += 1
            tok = (("d", eng, lane), 16 * (j // DMA_LANES + 1))
        else:
            self.eng_n[eng] += 1
            tok = (("e", eng), self.eng_n[eng])
        waits = {}
        for key, val in deps:
            if val > self.waited[eng].get(key, 0) and val > waits.get(key, 0):
                waits[key] = val
        for k, v in waits.items():
            self.waited[eng][k] = v
        self.items[eng].append((waits, emit, tok))
        for b in reads:
            b.r.append(tok)
        for b in writes:
            b.w = [tok]
            b.r = []
        return tok

    def all_tokens(self):
        toks = []
        for e in ENGINES:
            if self.eng_n[e]:
                toks.append((("e", e), self.eng_n[e]))
            for lane in range(DMA_LANES):
                n = (self.dma_n[e] + DMA_LANES - 1 - lane) // DMA_LANES
                if n:
                    toks.append((("d", e, lane), 16 * n))
        return toks

    def emit(self, nc, es):
        sems = {}
        for e in ENGINES:
            sems[("e", e)] = es.enter_context(nc.semaphore(f"se_{e}"))
            for lane in range(DMA_LANES):
                sems[("d", e, lane)] = es.enter_context(
                    nc.semaphore(f"sd_{e}{lane}"))
        block = es.enter_context(nc.Block())

        def mk(eng):
            def run(handle):
                for waits, emit, tok in self.items[eng]:
                    for key, val in waits.items():
                        handle.wait_ge(sems[key], val)
                    ins = emit(handle)
                    key, _ = tok
                    ins.then_inc(sems[key], 16 if key[0] == "d" else 1)
                for lane in range(DMA_LANES):
                    n = (self.dma_n[eng] + DMA_LANES - 1 - lane) // DMA_LANES
                    if n:
                        handle.wait_ge(sems[("d", eng, lane)], 16 * n)
            return run

        runs = {e: mk(e) for e in ENGINES}
        block.tensor(runs["tensor"])
        block.vector(runs["vector"])
        block.scalar(runs["scalar"])
        block.gpsimd(runs["gpsimd"])
        block.sync(runs["sync"])


class CT:

    def __init__(self, t, nb=1, seed=()):
        self.t = t
        self.bufs = [Buf(seed) for _ in range(nb)]

    def __getitem__(self, i):
        return self.bufs[i]

    @property
    def b(self):
        return self.bufs[0]


def build_program(n_layers=4):
    nc = bass.Bass("TRN2", target_bir_lowering=False)
    p = Prog()

    def din(name, shape, dt=F32):
        return nc.dram_tensor(name, shape, dt, kind="ExternalInput").ap()

    x_in = din("x_cm", [2, D, S])
    y_out = nc.dram_tensor("y_cm", [2, D, S], F32, kind="ExternalOutput").ap()

    W_qkv0 = din("W_qkv0", [D, 3 * D])
    bqkv0 = din("bqkv0", [128, 24])
    vconv = din("vconv", [128, 24])
    ffnW = {}
    for li in (0, 1):
        for nm in ("r", "k", "v"):
            ffnW[(li, nm)] = din(f"ffn{li}_W{nm}T", [D, D])
        ffnW[(li, "mur")] = din(f"ffn{li}_mur", [128, 8])
        ffnW[(li, "muk")] = din(f"ffn{li}_muk", [128, 8])
    Wk1 = din("Wk1", [D, D])
    Wv1 = din("Wv1", [D, D])
    bk1 = din("bk1", [128, 8])
    bv1 = din("bv1", [128, 8])
    RHmat = din("RHmat", [D, H])
    bidir = din("bidir", [1, 128])
    selHP = din("selHP", [H, 1024])
    Wqkv2 = din("Wqkv2", [D, 3 * D])
    Wout2 = din("Wout2", [D, D])
    bout2 = din("bout2", [128, 8])
    cosA_d = din("cosA", [128, S])
    sinA_d = din("sinA", [128, S])
    Pswap_d = din("Pswap", [128, 128])
    wgT2 = din("wgT2", [D, 3 * D], I32)
    wuT2 = din("wuT2", [D, 3 * D], I32)
    wdT2 = din("wdT2", [3 * D, D], I32)
    wqT3 = din("wqT3", [D, D])
    wkT3 = din("wkT3", [D, D])
    wvT3 = din("wvT3", [D, D])
    woT3 = din("woT3", [D, D])
    bq3 = din("bq3", [128, 8])
    bk3 = din("bk3", [128, 8])
    bo3 = din("bo3", [128, 8])
    bv3row = din("bv3row", [1, D])
    T2 = din("T2", [H, HD, 4095])
    wg3T = din("wg3T", [D, 4 * D])
    wu3T = din("wu3T", [D, 4 * D])
    wd3T = din("wd3T", [4 * D, D])

    RELW = 2560
    relscr = nc.dram_tensor("relscr", [S, RELW], BF16)
    wg3q = nc.dram_tensor("wg3q", [D, 4 * D], BF16).ap()
    s3scr = nc.dram_tensor("s3scr", [D, S], BF16).ap()
    wu3q = nc.dram_tensor("wu3q", [D, 4 * D], BF16).ap()
    wd3q = nc.dram_tensor("wd3q", [4 * D, D], BF16).ap()

    es = ExitStack()

    def sb(name, shape, dt):
        return es.enter_context(nc.sbuf_tensor(name, shape, dt))

    XR = CT(sb("XR", [128, NCH * S], BF16), NCH)
    XN = CT(sb("XN", [128, NCH * XNW], BF16), NCH)
    ident = CT(sb("ident", [128, 128], BF16))
    MS = CT(sb("MS", [128, 512], BF16))
    DS = CT(sb("DS", [128, 512], BF16))
    MS2 = CT(sb("MS2", [128, 1024], BF16))
    ones_c128 = CT(sb("ones_c128", [128, 1], BF16))
    ones_c128f = CT(sb("ones_c128f", [128, 1], F32))
    ones_r128f = CT(sb("ones_r128f", [1, 128], F32))
    STAT = CT(sb("STAT", [1, 6144], BF16))
    ones_r128b = CT(sb("ones_r128b", [1, 128], BF16))
    SMALL = CT(sb("SMALL", [128, 256], F32), 8)
    SM2 = CT(sb("SM2", [128, 192], F32), 8)
    WST = CT(sb("WST", [128, 1024], F32), 2)

    _acnt = [0]
    RELQ = CT(None, 1)

    def anew(stack, name, shape, dt, nb=1):
        _acnt[0] += 1
        t = stack.enter_context(nc.sbuf_tensor(f"{name}{_acnt[0]}", shape, dt))
        return CT(t, nb, p.all_tokens())

    PS = [CT(es.enter_context(nc.psum_tensor(f"ps{i}", [128, 512], F32)))
          for i in range(7)]
    PT = CT(es.enter_context(nc.psum_tensor("psT", [128, 512], BF16)))

    _ROWS = [(6, 0), (6, 32), (6, 64), (5, 0), (5, 32), (5, 64), (4, 0), (4, 32)]

    def prow(i, c0=0, n=512):
        bank, part = _ROWS[i]
        return PS[bank], PS[bank].t[part:part + 1, c0:c0 + n]

    def dma(eng, out_ap, in_ap, reads=(), writes=()):
        return p.op(eng, lambda h, o=out_ap, i=in_ap: h.dma_start(out=o, in_=i),
                    reads=reads, writes=writes, dma=True)

    def mm(out_ap, lhsT_ap, rhs_ap, reads, wbuf, start, stop):
        skip = out_ap.shape[0] < 128
        return p.op("tensor",
                    lambda h, o=out_ap, l=lhsT_ap, r=rhs_ap, s=start, t=stop,
                    k=skip:
                    h.matmul(o, l, r, start=s, stop=t, skip_group_check=k),
                    reads=reads, writes=[wbuf])

    def act(out_ap, in_ap, func, reads, writes, bias=0.0, scale=1.0, accum=None):
        if func == AF.Copy and not isinstance(bias, float):
            func = AF.Identity
        return p.op("scalar",
                    lambda h, o=out_ap, i=in_ap, f=func, b=bias, s=scale, a=accum:
                    h.activation(o, i, f, bias=b, scale=s, accum_out=a),
                    reads=reads, writes=writes)

    def tt(op_, out_ap, a_ap, b_ap, reads, writes, eng="vector"):
        return p.op(eng,
                    lambda h, o=out_ap, a=a_ap, b=b_ap, oo=op_:
                    h.tensor_tensor(o, a, b, oo),
                    reads=reads, writes=writes)

    def ts(out_ap, in_ap, s1, reads, writes, op0=ALU.mult, s2=None, op1=None,
           accum=None, eng="vector"):
        def emit(h, o=out_ap, i=in_ap, a=s1, b=s2, o0=op0, o1=op1, ac=accum):
            if o1 is None:
                return h.tensor_scalar(o, i, a, b, o0, accum_out=ac)
            return h.tensor_scalar(o, i, a, b, o0, op1=o1, accum_out=ac)
        return p.op(eng, emit, reads=reads, writes=writes)

    def stt(out_ap, in0_ap, scal, in1_ap, op0, op1, reads, writes, accum=None):
        return p.op("vector",
                    lambda h, o=out_ap, i=in0_ap, s=scal, j=in1_ap, a=op0, b=op1,
                    ac=accum:
                    h.scalar_tensor_tensor(o, i, s, j, a, b, accum_out=ac),
                    reads=reads, writes=writes)

    def recip(out_ap, in_ap, reads, writes):
        def emit(h, o=out_ap, i=in_ap):
            with nc.allow_low_precision(reason="bf16 stat rows"):
                return h.reciprocal(o, i)
        return p.op("vector", emit, reads=reads, writes=writes)

    def copy(out_ap, in_ap, reads, writes, eng="vector"):
        return p.op(eng, lambda h, o=out_ap, i=in_ap: h.tensor_copy(o, i),
                    reads=reads, writes=writes)

    def memset(out_ap, val, writes, eng="vector"):
        return p.op(eng, lambda h, o=out_ap, v=val: h.memset(o, v), writes=writes)

    def scan(out_ap, d0_ap, d1_ap, init, op0, op1, reads, writes):
        return p.op("vector",
                    lambda h, o=out_ap, a=d0_ap, b=d1_ap, i=init, x=op0, y=op1:
                    h.tensor_tensor_scan(o, a, b, i, x, y),
                    reads=reads, writes=writes)

    def transp(out_ap, in_ap, reads, writes):
        return p.op("tensor",
                    lambda h, o=out_ap, i=in_ap: h.transpose(o, i, ident.t[:, :]),
                    reads=reads + [ident.b], writes=writes)

    def xr(c, c0=0, n=S):
        return XR.t[:, c * S + c0: c * S + c0 + n]

    def xn(c, c0=0, n=S):
        return XN.t[:, c * XNW + XNP + c0: c * XNW + XNP + c0 + n]

    def sschunk(c, c0=0, n=S):
        return SS.t[:, c * S + c0: c * S + c0 + n]

    def aa(c, c0=0, n=S):
        return AA.t[:, c * S + c0: c * S + c0 + n]

    def bbc(c, c0=0, n=S):
        return BB.t[:, c * S + c0: c * S + c0 + n]

    def build_consts():
        memset(ident.t[:, :], 1.0, [ident.b], eng="gpsimd")
        p.op("gpsimd",
             lambda h: h.affine_select(out=ident.t[:, :], in_=ident.t[:, :],
                                       pattern=[[1, 128]], compare_op=ALU.is_equal,
                                       fill=0.0, base=0, channel_multiplier=-1),
             reads=[], writes=[ident.b])
        memset(MS.t[:, :], 1.0, [MS.b], eng="gpsimd")
        p.op("gpsimd",
             lambda h: h.affine_select(out=MS.t[:, :], in_=MS.t[:, :],
                                       pattern=[[1, 512]], compare_op=ALU.is_ge,
                                       fill=0.0, base=0, channel_multiplier=-1),
             reads=[], writes=[MS.b])
        memset(DS.t[:, :], 1.0, [DS.b], eng="gpsimd")
        p.op("gpsimd",
             lambda h: h.affine_select(out=DS.t[:, :], in_=DS.t[:, :],
                                       pattern=[[1, 512]], compare_op=ALU.is_equal,
                                       fill=0.0, base=0, channel_multiplier=-1),
             reads=[], writes=[DS.b])
        memset(MS2.t[:, :], 1.0, [MS2.b], eng="gpsimd")
        p.op("gpsimd",
             lambda h: h.affine_select(out=MS2.t[:, :], in_=MS2.t[:, :],
                                       pattern=[[1, 1024]], compare_op=ALU.is_ge,
                                       fill=0.0, base=-512, channel_multiplier=-1),
             reads=[], writes=[MS2.b])
        memset(ones_c128.t[:, :], 1.0, [ones_c128.b], eng="gpsimd")
        memset(ones_c128f.t[:, :], 1.0, [ones_c128f.b], eng="gpsimd")
        memset(ones_r128f.t[:, :], 1.0, [ones_r128f.b], eng="gpsimd")
        memset(ones_r128b.t[:, :], 1.0, [ones_r128b.b], eng="gpsimd")

    smalls = {}

    def load_small(name, dram_ap, slot, pool=SMALL, rows=128, cols=None):
        cols = dram_ap.shape[-1] if cols is None else cols
        t = pool.t[0:rows, slot * 32: slot * 32 + cols]
        dma("sync", t, dram_ap, writes=[pool[slot % 8]])
        smalls[name] = (t, pool[slot % 8])
        return t

    wst_state = {"i": 0}

    def load_w_cast(dst_ap, dst_buf, dram_ap, scale_col=None, rows=128):
        if dram_ap.dtype == BF16:
            dma("sync", dst_ap, dram_ap, reads=[RELQ.b], writes=[dst_buf])
            return
        i = wst_state["i"] % 2
        wst_state["i"] += 1
        ncols = dram_ap.shape[-1]
        assert ncols <= 512
        stg = WST.t[0:rows, i * 512: i * 512 + ncols]
        if dram_ap.dtype == I32:
            stg = stg.bitcast(I32)
        dma("sync", stg, dram_ap, writes=[WST[i]])
        if scale_col is None:
            copy(dst_ap, stg, [WST[i]], [dst_buf], eng="gpsimd")
        else:
            sc, scb = scale_col
            ts(dst_ap, stg, sc, [WST[i], scb], [dst_buf], eng="gpsimd")

    def layernorm(sq_t, sq_buf):
        for c in range(NCH):
            memset(XN.t[:, c * XNW: c * XNW + XNP], 0.0, [XN[c]])
        for c in range(NCH):
            for tci in range(NT):
                tt(ALU.mult, sq_t, xr(c, tci * 512, 512), xr(c, tci * 512, 512),
                   [XR[c]], [sq_buf])
                ct1, r1 = prow(tci)
                mm(r1, ones_c128.t[:, :],
                   xr(c, tci * 512, 512), [XR[c], ones_c128.b], ct1.b,
                   c == 0, c == NCH - 1)
                ct2, r2 = prow(4 + tci)
                mm(r2, ones_c128.t[:, :],
                   sq_t, [sq_buf, ones_c128.b],
                   ct2.b, c == 0, c == NCH - 1)
        for tci in range(NT):
            ct1, r1 = prow(tci)
            ct2, r2 = prow(4 + tci)
            act(srow("mean"), r1, AF.Copy, [ct1.b], [STAT.b], scale=1.0 / D)
            act(srow("rstd"), r2, AF.Copy, [ct2.b], [STAT.b], scale=1.0 / D)
            tt(ALU.mult, srow("tmp"), srow("mean"), srow("mean"),
               [STAT.b], [STAT.b])
            tt(ALU.subtract, srow("rstd"), srow("rstd"), srow("tmp"),
               [STAT.b], [STAT.b])
            ts(srow("rstd"), srow("rstd"), 1e-5, [STAT.b], [STAT.b],
               op0=ALU.add, s2=1e-20, op1=ALU.max)
            act(srow("tmp"), srow("rstd"), AF.Sqrt, [STAT.b], [STAT.b])
            recip(srow("rstd"), srow("tmp"), [STAT.b], [STAT.b])
            tt(ALU.mult, srow("mrs"), srow("mean"), srow("rstd"),
               [STAT.b], [STAT.b])
            mm(PS[0].t[:, :], ones_r128b.t[:, :],
               srow("rstd"), [STAT.b, ones_r128b.b], PS[0].b, True, True)
            mm(PS[1].t[:, :], ones_r128b.t[:, :],
               srow("mrs"), [STAT.b, ones_r128b.b], PS[1].b, True, True)
            for c in range(NCH):
                tt(ALU.mult, sq_t, xr(c, tci * 512, 512), PS[0].t[:, :],
                   [XR[c], PS[0].b], [sq_buf])
                tt(ALU.subtract, xn(c, tci * 512, 512), sq_t,
                   PS[1].t[:, :], [sq_buf, PS[1].b], [XN[c]])

    def proj(w_dram, m0, n_m, dst_cb, wpool, src_fn=None, src_bufs=None,
             mgroup=4, scale_col=None, shifted_pair=None, wpool2=None):
        if src_fn is None:
            src_fn = xn
            src_bufs = XN.bufs
        wslot = wpool.t.shape[1] // NCH
        hm = max(1, wslot // 256)
        hw = hm * 128
        halves = [(g0, min(hm, n_m - g0)) for g0 in range(0, n_m, hm)]
        wb = {}
        for c in range(NCH):
            for hf in range(2):
                wb[(c, hf, 0)] = Buf(wpool[c].w)
                if wpool2 is not None:
                    wb[(c, hf, 1)] = Buf(wpool2[c].w)

        def issue_loads(gi):
            g0, gn = halves[gi]
            hf = gi % 2
            for c in range(NCH):
                load_w_cast(
                    wpool.t[:, c * wslot + hf * hw: c * wslot + hf * hw + gn * 128],
                    wb[(c, hf, 0)],
                    w_dram[c * 128:(c + 1) * 128,
                           (m0 + g0) * 128:(m0 + g0 + gn) * 128],
                    scale_col=None if scale_col is None else scale_col(c))
                if shifted_pair is not None:
                    w2d, sc2 = shifted_pair
                    load_w_cast(
                        wpool2.t[:, c * wslot + hf * hw: c * wslot + hf * hw + gn * 128],
                        wb[(c, hf, 1)],
                        w2d[c * 128:(c + 1) * 128,
                            (m0 + g0) * 128:(m0 + g0 + gn) * 128],
                        scale_col=sc2(c))

        issue_loads(0)
        for gi, (g0, gn) in enumerate(halves):
            hf = gi % 2
            if gi + 1 < len(halves):
                issue_loads(gi + 1)
            for ml in range(gn):
                m = g0 + ml
                for c in range(NCH):
                    for tc in range(NT):
                        ps = PS[tc % 4]
                        last = (c == NCH - 1) and shifted_pair is None
                        mm(ps.t[:, :],
                           wpool.t[:, c * wslot + hf * hw + ml * 128:
                                   c * wslot + hf * hw + (ml + 1) * 128],
                           src_fn(c, tc * 512, 512),
                           [wb[(c, hf, 0)], src_bufs[c]],
                           ps.b, c == 0, last)
                if shifted_pair is not None:
                    for c in range(NCH):
                        for tc in range(NT):
                            ps = PS[tc % 4]
                            mm(ps.t[:, :],
                               wpool2.t[:, c * wslot + hf * hw + ml * 128:
                                       c * wslot + hf * hw + (ml + 1) * 128],
                               src_fn(c, tc * 512 - 1, 512),
                               [wb[(c, hf, 1)], src_bufs[c]],
                               ps.b, False, c == NCH - 1)
                for tc in range(NT):
                    dst_cb(m, tc, PS[tc % 4])
        for c in range(NCH):
            toks = []
            for hf in range(2):
                for wi in (0, 1):
                    if (c, hf, wi) in wb:
                        toks += wb[(c, hf, wi)].w + wb[(c, hf, wi)].r
            wpool[c].w = toks
            wpool[c].r = []
            if wpool2 is not None:
                wpool2[c].w = list(toks)
                wpool2[c].r = []

    cur = {}

    def aa(c, c0=0, n=S):
        return cur["AA"].t[:, c * S + c0: c * S + c0 + n]

    def bbc(c, c0=0, n=S):
        return cur["BB"].t[:, c * S + c0: c * S + c0 + n]

    def sschunk(c, c0=0, n=S):
        return cur["SS"].t[:, c * S + c0: c * S + c0 + n]

    def srow(name, c0=0, n=512):
        base = {"mean": 0, "rstd": 512, "tmp": 1024, "mrs": 1536,
                "rz": 2048, "diag": 4096}[name]
        return STAT.t[0:1, base + c0: base + c0 + n]

    def layer0(bi):
        esL = ExitStack()
        AA = anew(esL, "AA", [128, NCH * S], BF16, NCH)
        cur["SS"] = AA

        es2 = ExitStack()
        BB = anew(es2, "BB", [128, NCH * S], BF16, NCH)
        BBfine = CT(BB.t, NCH * NKT)
        E0 = anew(es2, "E0", [128, 17408], BF16, NKT)
        SQ = anew(es2, "SQ", [128, 512], BF16, 1)
        WQ = anew(es2, "WQ", [128, 8 * 256], BF16, NCH)
        cur["AA"], cur["BB"] = AA, BB
        eoff = [0] * NKT
        o = 0
        for kt in range(NKT):
            eoff[kt] = o
            o += S - 128 * kt
        assert o == 17408

        def e0sl(kt, qc):
            if kt >= 4 * qc:
                dmask = 128 * kt - 512 * qc
                w = 512 - dmask
                return (E0.t[:, eoff[kt]: eoff[kt] + w], w, dmask)
            off = 512 * qc - 128 * kt
            return (E0.t[:, eoff[kt] + off: eoff[kt] + off + 512], 512, 0)

        layernorm(SQ.t[:, 0:512], SQ[0])
        if KPHASE <= 1:
            es2.close(); esL.close(); return

        bq = smalls["bqkv0"][0]
        bqb = smalls["bqkv0"][1]
        ts(SM2.t[:, 0:8], bq[:, 0:8], 1.0 / 32.0, [bqb], [SM2[0]])

        def q_cb(m, tc, ps):
            act(aa(m, tc * 512, 512), ps.t[:, :], AF.Copy, [ps.b, SM2[0]],
                [AA[m]], bias=SM2.t[:, m:m + 1], scale=1.0 / 32.0)

        def k_cb(m, tc, ps):
            act(bbc(m, tc * 512, 512), ps.t[:, :], AF.Copy, [ps.b, bqb],
                [BB[m]], bias=bq[:, 8 + m:9 + m])

        proj(W_qkv0, 0, 8, q_cb, wpool=WQ, mgroup=2)
        proj(W_qkv0, 8, 8, k_cb, wpool=WQ, mgroup=2)
        if KPHASE <= 2:
            es2.close(); esL.close(); return

        l0items = [(kt, qc) for kt in range(NKT)
                   for qc in range(kt // 4, NT)]

        def l0_s(i):
            kt, qc = l0items[i]
            ps = PS[i % 3]
            for c in range(NCH):
                mm(ps.t[:, :], bbc(c, kt * 128, 128), aa(c, qc * 512, 512),
                   [AA[c], BB[c]], ps.b, c == 0, c == NCH - 1)
            dst, w, dmask = e0sl(kt, qc)
            act(dst, ps.t[:, dmask:512], AF.Exp, [ps.b], [E0[kt]])
            if kt >= 4 * qc:
                tt(ALU.mult, dst, dst, MS.t[:, 0:w],
                   [E0[kt], MS.b], [E0[kt]])
                tt(ALU.mult, SQ.t[:, 0:w], dst,
                   DS.t[:, 0:w], [E0[kt], DS.b], [SQ[0]])
                ctd, rd = prow(4 + qc, dmask, w)
                mm(rd, ones_c128.t[:, :],
                   SQ.t[:, 0:w], [SQ[0], ones_c128.b], ctd.b,
                   kt == 4 * qc, kt == 4 * qc + 3)

        def l0_z(i):
            kt, qc = l0items[i]
            dst, w, dmask = e0sl(kt, qc)
            ctz, rz = prow(qc, dmask, 512 - dmask)
            mm(rz, ones_c128.t[:, :], dst,
               [E0[kt], ones_c128.b], ctz.b, kt == 0, kt == 4 * qc + 3)

        for i in range(len(l0items) + 1):
            if i < len(l0items):
                l0_s(i)
            if i >= 1:
                l0_z(i - 1)

        for tc in range(NT):
            ct1, r1 = prow(tc)
            ct2, r2 = prow(4 + tc)
            act(srow("rz", tc * 512, 512), r1,
                AF.Copy, [ct1.b], [STAT.b])
            act(srow("diag", tc * 512, 512), r2,
                AF.Copy, [ct2.b], [STAT.b])
        recip(srow("rz", 0, 2048), srow("rz", 0, 2048), [STAT.b], [STAT.b])

        if KPHASE <= 3:
            es2.close(); esL.close(); return
        def v_cb(m, tc, ps):
            act(aa(m, tc * 512, 512), ps.t[:, :], AF.Copy, [ps.b, bqb],
                [AA[m]], bias=bq[:, 16 + m:17 + m])

        proj(W_qkv0, 16, 8, v_cb, wpool=WQ, mgroup=2)

        vc = smalls["vconv"][0]
        vcb = smalls["vconv"][1]
        for c in range(NCH):
            ts(bbc(c), aa(c), vc[:, c * 3 + 1:c * 3 + 2], [AA[c], vcb], [BB[c]])
            stt(bbc(c, 1, S - 1), aa(c, 0, S - 1), vc[:, c * 3:c * 3 + 1],
                bbc(c, 1, S - 1), ALU.mult, ALU.add,
                [AA[c], vcb, BB[c]], [BB[c]])
            stt(bbc(c, 0, S - 1), aa(c, 1, S - 1), vc[:, c * 3 + 2:c * 3 + 3],
                bbc(c, 0, S - 1), ALU.mult, ALU.add,
                [AA[c], vcb, BB[c]], [BB[c]])
        for c in range(NCH):
            ts(aa(c, 0, S - 1), aa(c, 1, S - 1), vc[:, c * 3 + 2:c * 3 + 3],
               [AA[c], vcb], [AA[c]])
            memset(AA.t[:, c * S + S - 1: c * S + S], 0.0, [AA[c]])
        for i in range(NCH * NKT):
            BBfine.bufs[i].w = list(BB[i // NKT].w)
            BBfine.bufs[i].r = []

        for c in range(NCH):
            for kt in range(NKT):
                fb = BBfine.bufs[c * NKT + kt]
                transp(PT.t[:, 0:128], bbc(c, kt * 128, 128), [fb], [PT.b])
                act(bbc(c, kt * 128, 128), PT.t[:, 0:128], AF.Copy,
                    [PT.b], [fb])

        for qc in range(NT):
            mm(PS[5].t[:, :], ones_r128b.t[:, :],
               srow("rz", qc * 512, 512), [STAT.b, ones_r128b.b],
               PS[5].b, True, True)
            act(WQ.t[:, 0:512], PS[5].t[:, :], AF.Copy, [PS[5].b], [WQ[0]])
            mm(PS[4].t[:, :], ones_r128b.t[:, :],
               srow("diag", qc * 512, 512), [STAT.b, ones_r128b.b],
               PS[4].b, True, True)
            for c in range(NCH):
                ps = PS[c % 3]
                nkt_q = 4 * qc + 4
                for kt in range(nkt_q):
                    fb = BBfine.bufs[c * NKT + kt]
                    src_e, w, dmask = e0sl(kt, qc)
                    mm(ps.t[:, dmask:512], bbc(c, kt * 128, 128), src_e,
                       [fb, E0[kt]], ps.b, kt == 0, kt == nkt_q - 1)
                tt(ALU.mult, SQ.t[:, 0:512], aa(c, qc * 512, 512), PS[4].t[:, :],
                   [AA[c], PS[4].b, SQ[0]], [SQ[0]])
                tt(ALU.subtract, SQ.t[:, 0:512], ps.t[:, :], SQ.t[:, 0:512],
                   [ps.b, SQ[0]], [SQ[0]])
                tt(ALU.mult, aa(c, qc * 512, 512), SQ.t[:, 0:512],
                   WQ.t[:, 0:512], [SQ[0], WQ[0], AA[c]], [AA[c]])

        if KPHASE <= 4:
            es2.close(); esL.close(); return
        es2.close()
        ffn_update(0)
        esL.close()

    def layer1(bi):
        esL = ExitStack()
        AA = anew(esL, "AA", [128, NCH * S], BF16, NCH)
        cur["SS"] = AA

        es2 = ExitStack()
        VV = anew(es2, "VV", [128, NCH * S], BF16, NCH)
        WQ = anew(es2, "WQ", [128, 8 * 512], BF16, NCH)
        SQ = anew(es2, "SQ", [128, 512], BF16, 1)
        DV = anew(es2, "DV", [16, S], BF16, 1)
        RHf = anew(es2, "RHf", [128, 128], F32, 1)
        SEL = anew(es2, "SEL", [16, 1024], BF16, 1)
        BIC = anew(es2, "BIC", [128, 128], F32, 1)
        WCOL = anew(es2, "WCOL", [128, 80], F32, 1)
        KRL = anew(es2, "KRL", [128, 512], F32, 1)
        VB = anew(es2, "VB", [128, S], BF16, 1)
        NB = anew(es2, "NB", [128, S], BF16, 1)
        VBW = anew(es2, "VBW", [128, S], BF16, 1)
        NBW = anew(es2, "NBW", [128, S], BF16, 1)
        D1 = anew(es2, "D1", [128, S], BF16, 1)
        HA = anew(es2, "HA", [128, S + 1], BF16, 1)
        HB = anew(es2, "HB", [128, S + 1], BF16, 1)
        ACC = anew(es2, "ACC", [128, S], F32, 1)
        COLS = anew(es2, "COLS", [128, 8], F32, 1)
        cur["AA"], cur["BB"] = AA, VV

        layernorm(SQ.t[:, :], SQ[0])
        if KPHASE == 11:
            es2.close(); esL.close(); return

        for piece in range(2):
            i = piece % 2
            dma("sync", WST.t[0:16, i * 512:i * 512 + 512],
                selHP[:, piece * 512:piece * 512 + 512], writes=[WST[i]])
            copy(SEL.t[:, piece * 512:piece * 512 + 512],
                 WST.t[0:16, i * 512:i * 512 + 512], [WST[i]], [SEL.b],
                 eng="gpsimd")
        dma("sync", WST.t[0:1, 0:128], bidir, writes=[WST[0]])
        mm(PS[6].t[:, 0:128], ones_r128f.t[:, :], WST.t[0:1, 0:128],
           [WST[0], ones_r128f.b], PS[6].b, True, True)
        copy(BIC.t[:, :], PS[6].t[:, 0:128], [PS[6].b], [BIC.b])
        for c in range(NCH):
            dma("sync", WST.t[:, (c % 2) * 512:(c % 2) * 512 + 16],
                RHmat[c * 128:(c + 1) * 128, :], writes=[WST[c % 2]])
            copy(RHf.t[:, c * 16:(c + 1) * 16],
                 WST.t[:, (c % 2) * 512:(c % 2) * 512 + 16], [WST[c % 2]],
                 [RHf.b], eng="gpsimd")
        for hp in range(8):
            for ri in range(8):
                copy(WCOL.t[0:64, hp * 8 + ri:hp * 8 + ri + 1],
                     BIC.t[0:64, (2 * hp) * 8 + ri:(2 * hp) * 8 + ri + 1],
                     [BIC.b], [WCOL.b])
                copy(WCOL.t[64:128, hp * 8 + ri:hp * 8 + ri + 1],
                     BIC.t[64:128, (2 * hp + 1) * 8 + ri:(2 * hp + 1) * 8 + ri + 1],
                     [BIC.b], [WCOL.b])
        for hp in range(8):
            copy(WCOL.t[:, 64 + hp:65 + hp], WCOL.t[:, hp * 8:hp * 8 + 1],
                 [WCOL.b], [WCOL.b])
            for ri in range(1, 8):
                tt(ALU.add, WCOL.t[:, 64 + hp:65 + hp],
                   WCOL.t[:, 64 + hp:65 + hp],
                   WCOL.t[:, hp * 8 + ri:hp * 8 + ri + 1], [WCOL.b], [WCOL.b])

        bk = smalls["bk1"][0]
        bkb = smalls["bk1"][1]
        bv = smalls["bv1"][0]
        bvb = smalls["bv1"][1]

        def v_cb(m, tc, ps):
            act(VV.t[:, m * S + tc * 512: m * S + tc * 512 + 512], ps.t[:, :],
                AF.Relu, [ps.b, bvb], [VV[m]], bias=bv[:, m:m + 1])

        proj(Wv1, 0, 8, v_cb, wpool=WQ)
        if KPHASE == 12:
            es2.close(); esL.close(); return

        _DOTS = [(4, 0), (4, 32), (4, 64), (5, 0)]

        def pdot(tc, c0=0, n=512):
            bank, part = _DOTS[tc]
            return PS[bank], PS[bank].t[part:part + 16, c0:c0 + n]

        for m in range(NCH):
            for c in range(NCH):
                load_w_cast(WQ.t[:, c * 512: c * 512 + 128], WQ[c],
                            Wk1[c * 128:(c + 1) * 128, m * 128:(m + 1) * 128])
            for tc in range(NT):
                ps = PS[tc % 3]
                for c in range(NCH):
                    mm(ps.t[:, :], WQ.t[:, c * 512: c * 512 + 128],
                       xn(c, tc * 512, 512), [WQ[c], XN[c]],
                       ps.b, c == 0, c == NCH - 1)
                act(KRL.t[:, :], ps.t[:, :], AF.Relu, [ps.b, bkb], [KRL.b],
                    bias=bk[:, m:m + 1])
                ctd, rd = pdot(tc)
                mm(rd, RHf.t[:, m * 16:(m + 1) * 16], KRL.t[:, :],
                   [RHf.b, KRL.b], ctd.b, m == 0, m == NCH - 1)
        for tc in range(NT):
            ctd, rd = pdot(tc)
            ts(DV.t[:, tc * 512:tc * 512 + 512], rd, 0.5, [ctd.b], [DV.b],
               op0=ALU.is_gt)

        if KPHASE == 13:
            es2.close(); esL.close(); return
        for hp in range(8):
            vv = lambda c0=0, n=S: VV.t[:, hp * S + c0: hp * S + c0 + n]
            for tc in range(NT):
                mm(PS[6].t[:, :], SEL.t[:, hp * 128:hp * 128 + 128],
                   DV.t[:, tc * 512:tc * 512 + 512], [SEL.b, DV.b], PS[6].b,
                   True, True)
                act(VB.t[:, tc * 512:tc * 512 + 512], PS[6].t[:, :], AF.Copy,
                    [PS[6].b], [VB.b])
            memset(VB.t[:, 0:1], 0.0, [VB.b])
            ts(NB.t[:, :], VB.t[:, :], -1.0, [VB.b], [NB.b], s2=1.0,
               op1=ALU.add)
            copy(VBW.t[:, :], VB.t[:, :], [VB.b], [VBW.b])
            memset(VBW.t[:, S - 1:S], 0.0, [VBW.b])
            ts(NBW.t[:, :], VBW.t[:, :], -1.0, [VBW.b], [NBW.b], s2=1.0,
               op1=ALU.add)
            memset(HA.t[:, 0:1], 0.0, [HA.b])
            memset(HA.t[:, S:S + 1], 0.0, [HA.b])
            memset(HB.t[:, 0:1], 0.0, [HB.b])
            memset(HB.t[:, S:S + 1], 0.0, [HB.b])

            wc = lambda ri: WCOL.t[:, hp * 8 + ri:hp * 8 + ri + 1]

            tt(ALU.mult, D1.t[:, :], VB.t[:, :], vv(), [VB.b, VV[hp]], [D1.b])
            scan(HA.t[:, 1:S + 1], NB.t[:, :], D1.t[:, :], vv(0, 1),
                 ALU.mult, ALU.add, [NB.b, D1.b, VV[hp]], [HA.b])
            ts(ACC.t[:, :], HA.t[:, 1:S + 1], wc(0), [HA.b, WCOL.b], [ACC.b])
            hcur, hnxt = HA, HB
            for ri in range(1, 4):
                tt(ALU.mult, D1.t[:, :], VB.t[:, :], hcur.t[:, 0:S],
                   [VB.b, hcur.b], [D1.b])
                scan(hnxt.t[:, 1:S + 1], NB.t[:, :], D1.t[:, :], vv(0, 1),
                     ALU.mult, ALU.add, [NB.b, D1.b, VV[hp]], [hnxt.b])
                stt(ACC.t[:, :], hnxt.t[:, 1:S + 1], wc(ri), ACC.t[:, :],
                    ALU.mult, ALU.add, [hnxt.b, WCOL.b, ACC.b], [ACC.b])
                hcur, hnxt = hnxt, hcur

            tt(ALU.mult, COLS.t[:, 0:1], VB.t[:, S - 1:S], vv(S - 1, 1),
               [VB.b, VV[hp]], [COLS.b])
            tt(ALU.mult, COLS.t[:, 1:2], NB.t[:, S - 1:S], vv(0, 1),
               [NB.b, VV[hp]], [COLS.b])
            tt(ALU.add, COLS.t[:, 0:1], COLS.t[:, 0:1], COLS.t[:, 1:2],
               [COLS.b], [COLS.b])
            tt(ALU.mult, D1.t[:, 0:S - 1], VBW.t[:, 0:S - 1], vv(1, S - 1),
               [VBW.b, VV[hp]], [D1.b])
            memset(D1.t[:, S - 1:S], 0.0, [D1.b])
            scan(HA.t[:, 0:S][:, ::-1], NBW.t[:, ::-1], D1.t[:, ::-1],
                 COLS.t[:, 0:1], ALU.mult, ALU.add,
                 [NBW.b, D1.b, COLS.b], [HA.b])
            stt(ACC.t[:, :], HA.t[:, 0:S], wc(4), ACC.t[:, :],
                ALU.mult, ALU.add, [HA.b, WCOL.b, ACC.b], [ACC.b])
            hcur, hnxt = HA, HB
            for ri in range(1, 4):
                tt(ALU.mult, D1.t[:, :], VBW.t[:, :], hcur.t[:, 1:S + 1],
                   [VBW.b, hcur.b], [D1.b])
                scan(hnxt.t[:, 0:S][:, ::-1], NBW.t[:, ::-1], D1.t[:, ::-1],
                     vv(0, 1), ALU.mult, ALU.add,
                     [NBW.b, D1.b, VV[hp]], [hnxt.b])
                stt(ACC.t[:, :], hnxt.t[:, 0:S], wc(4 + ri), ACC.t[:, :],
                    ALU.mult, ALU.add, [hnxt.b, WCOL.b, ACC.b], [ACC.b])
                hcur, hnxt = hnxt, hcur

            tt(ALU.mult, ACC.t[:, 0:1], WCOL.t[:, 64 + hp:65 + hp], vv(0, 1),
               [WCOL.b, VV[hp]], [ACC.b])
            copy(AA.t[:, hp * S:(hp + 1) * S], ACC.t[:, :], [ACC.b], [AA[hp]])

        es2.close()
        ffn_update(1)
        esL.close()

    def layer2(bi):
        esL = ExitStack()
        AA = anew(esL, "AA", [128, NCH * S], BF16, NCH)
        cur["AA"] = AA

        es2 = ExitStack()
        KK = anew(es2, "KK", [128, NCH * S], BF16, NCH)
        VT = anew(es2, "VT", [128, NKT * 1024], BF16, NKT)
        SQ = anew(es2, "SQ", [128, 512], BF16, 1)
        ED = anew(es2, "ED", [128, 2048], BF16, 4)
        COS = anew(es2, "COS", [128, S], BF16, 1)
        SIN = anew(es2, "SIN", [128, S], BF16, 1)
        PSW = anew(es2, "PSW", [128, 128], BF16, 1)
        SQ2 = anew(es2, "SQ2", [128, 1024], BF16, 1)
        es2a = ExitStack()
        WQ = anew(es2a, "WQ", [128, 8 * 512], BF16, NCH)
        cur["BB"] = KK

        layernorm(SQ.t[:, :], SQ[0])

        for piece in range(4):
            i = piece % 2
            dma("sync", WST.t[:, i * 512:i * 512 + 512],
                cosA_d[:, piece * 512:piece * 512 + 512], writes=[WST[i]])
            copy(COS.t[:, piece * 512:piece * 512 + 512],
                 WST.t[:, i * 512:i * 512 + 512], [WST[i]], [COS.b],
                 eng="gpsimd")
            dma("sync", WST.t[:, i * 512:i * 512 + 512],
                sinA_d[:, piece * 512:piece * 512 + 512], writes=[WST[i]])
            copy(SIN.t[:, piece * 512:piece * 512 + 512],
                 WST.t[:, i * 512:i * 512 + 512], [WST[i]], [SIN.b],
                 eng="gpsimd")
        dma("sync", WST.t[:, 0:128], Pswap_d, writes=[WST[0]])
        copy(PSW.t[:, :], WST.t[:, 0:128], [WST[0]], [PSW.b], eng="gpsimd")

        def q_cb(m, tc, ps):
            act(aa(m, tc * 512, 512), ps.t[:, :], AF.Copy, [ps.b], [AA[m]],
                scale=0.125)

        def k_cb(m, tc, ps):
            act(KK.t[:, m * S + tc * 512: m * S + tc * 512 + 512], ps.t[:, :],
                AF.Copy, [ps.b], [KK[m]])

        proj(Wqkv2, 0, 8, q_cb, wpool=WQ)
        proj(Wqkv2, 8, 8, k_cb, wpool=WQ)

        for vc in range(2):
            for c in range(NCH):
                load_w_cast(WQ.t[:, c * 512: c * 512 + 512], WQ[c],
                            Wqkv2[c * 128:(c + 1) * 128,
                                  2048 + vc * 512: 2048 + vc * 512 + 512])
            for tt_ in range(NKT):
                ps = PS[tt_ % 2]
                for c in range(NCH):
                    mm(ps.t[:, :], xn(c, tt_ * 128, 128),
                       WQ.t[:, c * 512: c * 512 + 512], [XN[c], WQ[c]],
                       ps.b, c == 0, c == NCH - 1)
                act(VT.t[:, tt_ * 1024 + vc * 512: tt_ * 1024 + vc * 512 + 512],
                    ps.t[:, :], AF.Copy, [ps.b], [VT[tt_]])

        es2a.close()
        DENR = anew(es2, "DENR", [1, 2 * S], BF16, 1)
        for hp in range(8):
            for hh in range(2):
                pb = 64 * hh
                NI = NKT * NT

                def den_s(i):
                    kt, qc = divmod(i, NT)
                    ps = PS[i % 3]
                    mm(ps.t[:, :],
                       KK.t[pb:pb + 64, hp * S + kt * 128:
                            hp * S + kt * 128 + 128],
                       AA.t[pb:pb + 64, hp * S + qc * 512:
                            hp * S + qc * 512 + 512],
                       [KK[hp], AA[hp]], ps.b, True, True)
                    eb = ED.bufs[i % 4]
                    et = ED.t[:, (i % 4) * 512:(i % 4) * 512 + 512]
                    act(et, ps.t[:, :], AF.Exp, [ps.b], [eb])

                def den_z(i):
                    kt, qc = divmod(i, NT)
                    eb = ED.bufs[i % 4]
                    et = ED.t[:, (i % 4) * 512:(i % 4) * 512 + 512]
                    ctz, rz = prow(qc)
                    mm(rz, ones_c128.t[:, :], et, [eb, ones_c128.b],
                       ctz.b, kt == 0, kt == NKT - 1)

                for i in range(NI + 2):
                    if i < NI:
                        den_s(i)
                    if i >= 2:
                        den_z(i - 2)
                for qc in range(NT):
                    ctz, rz = prow(qc)
                    act(DENR.t[0:1, hh * S + qc * 512: hh * S + qc * 512 + 512],
                        rz, AF.Copy, [ctz.b], [DENR.b])
            recip(DENR.t[0:1, :], DENR.t[0:1, :], [DENR.b], [DENR.b])

            for tc in range(NT):
                for j_, (TT_, TB) in enumerate(((AA, AA[hp]), (KK, KK[hp]))):
                    psr = PS[5] if (tc * 2 + j_) % 2 == 0 else PS[4]
                    base = hp * S + tc * 512
                    mm(psr.t[:, :], PSW.t[:, :], TT_.t[:, base:base + 512],
                       [PSW.b, TB], psr.b, True, True)
                    tt(ALU.mult, SQ2.t[:, 0:512], TT_.t[:, base:base + 512],
                       COS.t[:, tc * 512:tc * 512 + 512], [TB, COS.b], [SQ2[0]])
                    tt(ALU.mult, SQ2.t[:, 512:1024], psr.t[:, :],
                       SIN.t[:, tc * 512:tc * 512 + 512],
                       [psr.b, SIN.b], [SQ2[0]])
                    tt(ALU.add, TT_.t[:, base:base + 512], SQ2.t[:, 0:512],
                       SQ2.t[:, 512:1024], [SQ2[0], TB], [TB])

            for hh in range(2):
                h = 2 * hp + hh
                pb = 64 * hh
                for qc in range(NT):
                    psc = PS[6] if qc % 2 == 0 else PS[3]
                    nkt_q = 4 * qc + 4

                    def num_s(kt):
                        ps = PS[kt % 3]
                        mm(ps.t[:, :],
                           KK.t[pb:pb + 64, hp * S + kt * 128:
                                hp * S + kt * 128 + 128],
                           AA.t[pb:pb + 64, hp * S + qc * 512:
                                hp * S + qc * 512 + 512],
                           [KK[hp], AA[hp]], ps.b, True, True)
                        eb = ED.bufs[kt % 4]
                        et = ED.t[:, (kt % 4) * 512:(kt % 4) * 512 + 512]
                        act(et, ps.t[:, :], AF.Exp, [ps.b], [eb])
                        if kt >= 4 * qc:
                            dmask = 128 * kt - 512 * qc
                            tt(ALU.mult, et, et,
                               MS2.t[:, 512 - dmask:1024 - dmask],
                               [eb, MS2.b], [eb])

                    def num_av(kt):
                        eb = ED.bufs[kt % 4]
                        et = ED.t[:, (kt % 4) * 512:(kt % 4) * 512 + 512]
                        mm(psc.t[pb:pb + 64, :],
                           VT.t[:, kt * 1024 + h * 64:kt * 1024 + h * 64 + 64],
                           et, [VT[kt], eb], psc.b, kt == 0, kt == nkt_q - 1)

                    for kt in range(nkt_q + 2):
                        if kt < nkt_q:
                            num_s(kt)
                        if kt >= 2:
                            num_av(kt - 2)
                    psr = PS[5] if qc % 2 == 0 else PS[4]
                    mm(psr.t[0:64, :], ones_r128b.t[0:1, 0:64],
                       DENR.t[0:1, hh * S + qc * 512:hh * S + qc * 512 + 512],
                       [DENR.b, ones_r128b.b], psr.b, True, True)
                    act(SQ2.t[0:64, 0:512], psr.t[0:64, :], AF.Copy,
                        [psr.b], [SQ2[0]])
                    tt(ALU.mult,
                       AA.t[pb:pb + 64,
                            hp * S + qc * 512:hp * S + qc * 512 + 512],
                       psc.t[pb:pb + 64, :], SQ2.t[0:64, 0:512],
                       [psc.b, SQ2[0], AA[hp]], [AA[hp]])

        es2.close()

        SSn = anew(esL, "SS", [128, NCH * S], BF16, NCH)
        WQb = anew(esL, "WQb", [128, 8 * 512], BF16, NCH)
        cur["SS"] = SSn
        bo = smalls["bout2"][0]
        bob = smalls["bout2"][1]

        def s_cb(m, tc, ps):
            act(SSn.t[:, m * S + tc * 512: m * S + tc * 512 + 512], ps.t[:, :],
                AF.Copy, [ps.b, bob], [SSn[m]], bias=bo[:, m:m + 1])

        proj(Wout2, 0, 8, s_cb, wpool=WQb, src_fn=aa, src_bufs=AA.bufs)
        ffn2(bi)
        esL.close()

    def ffn2(bi):
        es2 = ExitStack()
        TG = 512
        HH = anew(es2, "HH", [128, 24 * TG], BF16, 24)
        WG = anew(es2, "WG", [128, 8 * 512], BF16, NCH)
        WU = anew(es2, "WU", [128, 8 * 512], BF16, NCH)
        SQ = anew(es2, "SQ", [128, 1024], BF16, 1)
        SS = cur["SS"]
        for tg in range(S // TG):
            t0 = tg * TG
            ntc = TG // 512
            for g0 in range(0, 24, 4):
                for c in range(NCH):
                    load_w_cast(WG.t[:, c * 512: c * 512 + 512], WG[c],
                                wgT2[c * 128:(c + 1) * 128,
                                     g0 * 128:(g0 + 4) * 128])
                    load_w_cast(WU.t[:, c * 512: c * 512 + 512], WU[c],
                                wuT2[c * 128:(c + 1) * 128,
                                     g0 * 128:(g0 + 4) * 128])
                for ml in range(4):
                    m = g0 + ml
                    pg = PS[ml % 2]
                    pu = PS[2 + ml % 2]
                    for c in range(NCH):
                        mm(pg.t[:, :],
                           WG.t[:, c * 512 + ml * 128: c * 512 + (ml + 1) * 128],
                           xn(c, t0, 512), [WG[c], XN[c]],
                           pg.b, c == 0, c == NCH - 1)
                    act(HH.t[:, m * TG: m * TG + 512],
                        pg.t[:, :], AF.Silu, [pg.b], [HH[m]],
                        scale=1.0 / 127.0)
                    for c in range(NCH):
                        mm(pu.t[:, :],
                           WU.t[:, c * 512 + ml * 128: c * 512 + (ml + 1) * 128],
                           xn(c, t0, 512), [WU[c], XN[c]],
                           pu.b, c == 0, c == NCH - 1)
                    i2 = ml % 2
                    act(SQ.t[:, i2 * 512:i2 * 512 + 512], pu.t[:, :], AF.Copy,
                        [pu.b], [SQ[0]], scale=1.0 / 127.0)
                    tt(ALU.mult,
                       HH.t[:, m * TG: m * TG + 512],
                       HH.t[:, m * TG: m * TG + 512],
                       SQ.t[:, i2 * 512:i2 * 512 + 512], [SQ[0], HH[m]], [HH[m]])
            for m in range(NCH):
                for g0 in range(0, 24, 8):
                    for cc in range(8):
                        load_w_cast(WG.t[:, cc * 512: cc * 512 + 128],
                                    WG[cc],
                                    wdT2[(g0 + cc) * 128:(g0 + cc + 1) * 128,
                                         m * 128:(m + 1) * 128])
                    for cc in range(8):
                        ci = g0 + cc
                        mm(PS[m % 3].t[:, :],
                           WG.t[:, cc * 512: cc * 512 + 128],
                           HH.t[:, ci * TG: ci * TG + 512],
                           [WG[cc], HH[ci]],
                           PS[m % 3].b, ci == 0, ci == 23)
                ps = PS[m % 3]
                act(SQ.t[:, 512:1024], ps.t[:, :], AF.Silu, [ps.b], [SQ[0]],
                    scale=1.0 / 127.0)
                tt(ALU.mult, SQ.t[:, 0:512], SQ.t[:, 512:1024],
                   SS.t[:, m * S + t0: m * S + t0 + 512],
                   [SQ[0], SS[m]], [SQ[0]])
                tt(ALU.add, xr(m, t0, 512), xr(m, t0, 512),
                   SQ.t[:, 0:512], [SQ[0], XR[m]], [XR[m]])
        es2.close()

    def layer3(bi):
        esL = ExitStack()
        AA = anew(esL, "AA", [128, NCH * S], BF16, NCH)
        cur["AA"] = AA

        es2 = ExitStack()
        KK = anew(es2, "KK", [128, NCH * S], BF16, NCH)
        VT = anew(es2, "VT", [128, NKT * 1040], BF16, NKT)
        SQ = anew(es2, "SQ", [128, 1024], BF16, 2)
        es2a = ExitStack()
        WQ = anew(es2a, "WQ", [128, 8 * 512], BF16, NCH)
        BVR = anew(es2a, "BVR", [1, 1024], BF16, 1)
        cur["BB"] = KK

        layernorm(SQ.t[:, 0:512], SQ[0])

        bq = smalls["bq3"][0]
        bqb = smalls["bq3"][1]
        bk = smalls["bk3"][0]
        bkb = smalls["bk3"][1]
        ts(SM2.t[:, 40:48], bk, 0.125, [bkb], [SM2[2]])

        def q_cb(m, tc, ps):
            act(aa(m, tc * 512, 512), ps.t[:, :], AF.Copy, [ps.b, bqb],
                [AA[m]], bias=bq[:, m:m + 1])

        def k_cb(m, tc, ps):
            act(KK.t[:, m * S + tc * 512: m * S + tc * 512 + 512], ps.t[:, :],
                AF.Copy, [ps.b, SM2[2]], [KK[m]], bias=SM2.t[:, 40 + m:41 + m],
                scale=0.125)

        proj(wqT3, 0, 8, q_cb, wpool=WQ)
        proj(wkT3, 0, 8, k_cb, wpool=WQ)

        dma("sync", WST.t[0:1, 0:1024], bv3row, writes=[WST[0], WST[1]])
        copy(BVR.t[0:1, :], WST.t[0:1, 0:1024], [WST[0]], [BVR.b], eng="gpsimd")
        for vc in range(2):
            for c in range(NCH):
                load_w_cast(WQ.t[:, c * 512: c * 512 + 512], WQ[c],
                            wvT3[c * 128:(c + 1) * 128,
                                 vc * 512: vc * 512 + 512])
            for tt_ in range(NKT):
                ps = PS[tt_ % 2]
                for c in range(NCH):
                    mm(ps.t[:, :], xn(c, tt_ * 128, 128),
                       WQ.t[:, c * 512: c * 512 + 512], [XN[c], WQ[c]],
                       ps.b, c == 0, False)
                mm(ps.t[:, :], ones_r128b.t[0:1, :],
                   BVR.t[0:1, vc * 512: vc * 512 + 512], [BVR.b, ones_r128b.b],
                   ps.b, False, True)
                hbase = vc * 8 * 65
                dst = bass.AP(VT.t, tt_ * 1040 + hbase,
                              [[NKT * 1040, 128], [65, 8], [1, 64]])
                act(dst, ps.t[:, :], AF.Copy, [ps.b], [VT[tt_]])
        for tt_ in range(NKT):
            ones_ap = bass.AP(VT.t, tt_ * 1040 + 64,
                              [[NKT * 1040, 128], [65, 16], [1, 1]])
            memset(ones_ap, 1.0, [VT[tt_]])
        es2a.close()

        RELB = CT(None, 1)

        for h in range(H):
            hp, hh = h // 2, h % 2
            pb = 64 * hh
            es2b = ExitStack()
            TTB = anew(es2b, "TTB", [128, 4096], BF16, 1)
            BR = anew(es2b, "BR", [128, 2560], BF16, 1)

            for piece in range(8):
                i = piece % 2
                n = 512 if piece < 7 else 4095 - 7 * 512
                dma("sync", WST.t[0:64, i * 512: i * 512 + n],
                    T2[h, :, piece * 512: piece * 512 + n], writes=[WST[i]])
                copy(TTB.t[0:64, piece * 512: piece * 512 + n],
                     WST.t[0:64, i * 512: i * 512 + n], [WST[i]], [TTB.b],
                     eng="gpsimd")
                dma("gpsimd", TTB.t[64:128, piece * 512: piece * 512 + n],
                    WST.t[0:64, i * 512: i * 512 + n], reads=[WST[i]],
                    writes=[TTB.b])
            memset(TTB.t[:, 4095:4096], 0.0, [TTB.b])

            for qt in range(NKT):
                a0 = 1920 - 128 * qt
                A0 = (a0 // 512) * 512
                for ch in range(5):
                    ps = PS[ch % 2]
                    i0 = A0 + ch * 512
                    mm(ps.t[:, :],
                       AA.t[pb:pb + 64, hp * S + qt * 128: hp * S + qt * 128 + 128],
                       TTB.t[pb:pb + 64, i0:i0 + 512], [AA[hp], TTB.b],
                       ps.b, True, True)
                    act(BR.t[:, ch * 512:(ch + 1) * 512], ps.t[:, :], AF.Exp,
                        [ps.b], [BR.b])
                dma("sync",
                    bass.AP(relscr, qt * 128 * 2560, [[2560, 128], [1, 2560]]),
                    BR.t[:, :], reads=[BR.b], writes=[RELB.b])
            es2b.close()
            es2c = ExitStack()
            XPR = anew(es2c, "XPR", [128, S], BF16, 1)
            E4 = anew(es2c, "E4", [128, 4 * S], BF16, 4)
            ET = anew(es2c, "ET", [128, 1024], BF16, 2)
            rzr_t = srow("rz", 0, 512)

            for qc in range(NT):
                psc = PS[6] if qc % 2 == 0 else PS[3]
                for qi in range(4):
                    qt = 4 * qc + qi
                    a0 = 1920 - 128 * qt
                    A0 = (a0 // 512) * 512
                    coff = qt * 128 * 2560 + 2047 - qt * 128 - A0
                    dma("sync", XPR.t[:, :],
                        bass.AP(relscr, coff, [[2559, 128], [1, 2048]]),
                        reads=[RELB.b], writes=[XPR.b])
                    for kc in range(NT):
                        ps = PS[(0, 1, 2, 4)[kc]]
                        mm(ps.t[:, :],
                           AA.t[pb:pb + 64,
                                hp * S + qt * 128: hp * S + qt * 128 + 128],
                           KK.t[pb:pb + 64,
                                hp * S + kc * 512: hp * S + kc * 512 + 512],
                           [AA[hp], KK[hp]], ps.b, True, True)
                        sqb = SQ.bufs[kc % 2]
                        sqt = SQ.t[:, (kc % 2) * 512:(kc % 2) * 512 + 512]
                        act(sqt, ps.t[:, :], AF.Exp, [ps.b], [sqb])
                        tt(ALU.mult,
                           E4.t[:, qi * S + kc * 512: qi * S + kc * 512 + 512],
                           sqt, XPR.t[:, kc * 512: kc * 512 + 512],
                           [sqb, XPR.b], [E4[qi]])
                for kt in range(NKT):
                    eb = ET.bufs[kt % 2]
                    et = ET.t[:, (kt % 2) * 512:(kt % 2) * 512 + 512]
                    for qi in range(4):
                        transp(PT.t[:, qi * 128:(qi + 1) * 128],
                               E4.t[:, qi * S + kt * 128: qi * S + kt * 128 + 128],
                               [E4[qi]], [PT.b])
                    copy(et, PT.t[:, :], [PT.b], [eb], eng="vector")
                    mm(psc.t[0:65, :],
                       VT.t[:, kt * 1040 + h * 65: kt * 1040 + h * 65 + 65],
                       et, [VT[kt], eb], psc.b, kt == 0, kt == NKT - 1)
                act(rzr_t, psc.t[64:65, :], AF.Copy, [psc.b], [STAT.b])
                recip(rzr_t, rzr_t, [STAT.b], [STAT.b])
                mm(PS[5].t[0:64, :], ones_r128b.t[0:1, 0:64], rzr_t,
                   [STAT.b, ones_r128b.b], PS[5].b, True, True)
                act(SQ.t[0:64, 0:512], PS[5].t[0:64, :], AF.Copy,
                    [PS[5].b], [SQ[0]])
                tt(ALU.mult,
                   AA.t[pb:pb + 64, hp * S + qc * 512: hp * S + qc * 512 + 512],
                   psc.t[0:64, :], SQ.t[0:64, 0:512],
                   [psc.b, SQ[0], AA[hp]], [AA[hp]])
            es2c.close()

        es2.close()

        esW = ExitStack()
        WQb = anew(esW, "WQb", [128, 8 * 512], BF16, NCH)
        SEV = anew(esW, "SEV", [128, 1024], BF16, 2)
        SSCR = CT(None, 1)
        cur["S3"] = SSCR
        bo = smalls["bo3"][0]
        bob = smalls["bo3"][1]

        def s_cb(m, tc, ps):
            i = (m * NT + tc) % 2
            act(SEV.t[:, i * 512:(i + 1) * 512], ps.t[:, :],
                AF.Copy, [ps.b, bob], [SEV.bufs[i]], bias=bo[:, m:m + 1])
            dma("sync", s3scr[m * 128:(m + 1) * 128, tc * 512: tc * 512 + 512],
                SEV.t[:, i * 512:(i + 1) * 512], reads=[SEV.bufs[i]],
                writes=[SSCR.b])

        proj(woT3, 0, 8, s_cb, wpool=WQb, src_fn=aa, src_bufs=AA.bufs)
        esW.close()
        esL.close()
        ffn3(bi)

    def quant_weight_to_scratch(w_dram, n_k, n_m, scratch_dram, es_unused):
        es = ExitStack()
        QW = anew(es, "QW", [128, 8 * 512], F32, 8)
        QS = anew(es, "QS", [128, 1024], F32, 1)
        QB = anew(es, "QB", [128, 512], BF16, 1)
        QI = anew(es, "QI", [128, 512], I32, 1)
        QC = anew(es, "QC", [128, 8], F32, 1)
        QR = anew(es, "QR", [1, 8], F32, 1)
        npieces = n_m * 128 // 512
        idx = 0
        tot_pieces = n_k * npieces
        for c in range(n_k):
            for pc in range(npieces):
                i = idx % 8
                dma("gpsimd", QW.t[:, i * 512: i * 512 + 512],
                    w_dram[c * 128:(c + 1) * 128, pc * 512: pc * 512 + 512],
                    writes=[QW.bufs[i]])
                stt(QS.t[:, 0:512], QW.t[:, i * 512: i * 512 + 512], -1.0,
                    QW.t[:, i * 512: i * 512 + 512], ALU.mult, ALU.max,
                    [QW.bufs[i]], [QS.b])
                r = PS[3].t[0:1, 0:512]
                mm(r, ones_c128f.t[:, :], QS.t[:, 0:512],
                   [QS.b, ones_c128f.b], PS[3].b, idx == 0,
                   idx == tot_pieces - 1)
                idx += 1
        ct, r = PS[3], PS[3].t[0:1, 0:512]
        p.op("vector",
             lambda h_, o=QR.t[0:1, 0:1], i=r:
             h_.reduce_sum(o, i, axis=mybir.AxisListType.X),
             reads=[ct.b], writes=[QR.b])
        tot = float(n_k * 128 * n_m * 128)
        ts(QR.t[0:1, 1:2], QR.t[0:1, 0:1], 1.0 / tot, [QR.b], [QR.b],
           s2=1e-5, op1=ALU.max)
        recip(QR.t[0:1, 2:3], QR.t[0:1, 1:2], [QR.b], [QR.b])
        mm(PS[5].t[:, 0:2], ones_r128f.t[0:1, :],
           QR.t[0:1, 1:3], [QR.b, ones_r128f.b], PS[5].b, True, True)
        act(QC.t[:, 0:1], PS[5].t[:, 1:2], AF.Copy, [PS[5].b], [QC.b])
        act(QC.t[:, 1:2], PS[5].t[:, 0:1], AF.Copy, [PS[5].b], [QC.b])
        idx = 0
        for c in range(n_k):
            for pc in range(npieces):
                i = idx % 8
                idx += 1
                dma("gpsimd", QW.t[:, i * 512: i * 512 + 512],
                    w_dram[c * 128:(c + 1) * 128, pc * 512: pc * 512 + 512],
                    writes=[QW.bufs[i]])
                ts(QS.t[:, 0:512], QW.t[:, i * 512: i * 512 + 512],
                   QC.t[:, 0:1], [QW.bufs[i], QC.b], [QS.b])
                copy(QI.t[:, :], QS.t[:, 0:512], [QS.b], [QI.b])
                ts(QI.t[:, :], QI.t[:, :], 1, [QI.b], [QI.b],
                   op0=ALU.min, s2=-1, op1=ALU.max)
                ts(QB.t[:, :], QI.t[:, :],
                   QC.t[:, 1:2], [QI.b, QC.b], [QB.b])
                dma("gpsimd",
                    scratch_dram[c * 128:(c + 1) * 128, pc * 512: pc * 512 + 512],
                    QB.t[:, :], reads=[QB.b], writes=[RELQ.b])
        es.close()

    def ffn3(bi):
        es2 = ExitStack()
        SSCR = cur["S3"]
        SP = anew(es2, "SP", [128, 1024], BF16, 2)
        AM = anew(es2, "AM", [128, S], BF16, 1)
        MXR = anew(es2, "MXR", [1, 2 * S], BF16, 1)
        SQ = anew(es2, "SQ", [128, 1024], BF16, 1)
        QI = anew(es2, "QI", [128, 512], I32, 1)
        QF = anew(es2, "QF", [128, 512], F32, 1)
        MXT = anew(es2, "MXT", [128, 16], BF16, 1)

        def act_quant(src_fn, src_bufs, nch, width, dst_fn, dst_bufs):
            stt(AM.t[:, 0:width], src_fn(0, 0, width), -1.0,
                src_fn(0, 0, width), ALU.mult, ALU.max, [src_bufs[0]], [AM.b])
            for c in range(1, nch):
                for pp in range(0, width, 512):
                    n = min(512, width - pp)
                    stt(SQ.t[:, 0:n], src_fn(c, pp, n), -1.0, src_fn(c, pp, n),
                        ALU.mult, ALU.max, [src_bufs[c]], [SQ[0]])
                    tt(ALU.max, AM.t[:, pp:pp + n], AM.t[:, pp:pp + n],
                       SQ.t[:, 0:n], [SQ[0], AM.b], [AM.b])
            ntt = width // 128
            for b_ in range(ntt):
                transp(PT.t[:, 0:128], AM.t[:, b_ * 128:(b_ + 1) * 128],
                       [AM.b], [PT.b])
                act(SQ.t[:, 0:128], PT.t[:, 0:128], AF.Copy, [PT.b], [SQ[0]])
                p.op("vector",
                     lambda h_, o=MXT.t[:, b_:b_ + 1], i=SQ.t[:, 0:128]:
                     h_.reduce_max(o, i, axis=mybir.AxisListType.X),
                     reads=[SQ[0]], writes=[MXT.b])
            transp(PT.t[0:16, 0:128], MXT.t[:, 0:16], [MXT.b], [PT.b])
            copy(SQ.t[0:16, 512:640], PT.t[0:16, 0:128], [PT.b], [SQ[0]])
            for b_ in range(ntt):
                dma("gpsimd", MXR.t[0:1, b_ * 128:(b_ + 1) * 128],
                    SQ.t[b_:b_ + 1, 512:640], reads=[SQ[0]], writes=[MXR.b])
            ts(MXR.t[0:1, 0:width], MXR.t[0:1, 0:width], 1e-5,
               [MXR.b], [MXR.b], op0=ALU.max)
            ts(MXR.t[0:1, S:S + width], MXR.t[0:1, 0:width], 1.0 / 127.0,
               [MXR.b], [MXR.b])
            recip(MXR.t[0:1, 0:width], MXR.t[0:1, 0:width], [MXR.b], [MXR.b])
            ts(MXR.t[0:1, 0:width], MXR.t[0:1, 0:width], 127.0,
               [MXR.b], [MXR.b])
            ntc = (width + 511) // 512
            for tc in range(ntc):
                n = min(512, width - tc * 512)
                mm(PS[5].t[:, 0:n], ones_r128b.t[0:1, :],
                   MXR.t[0:1, tc * 512: tc * 512 + n], [MXR.b, ones_r128b.b],
                   PS[5].b, True, True)
                mm(PS[4].t[:, 0:n], ones_r128b.t[0:1, :],
                   MXR.t[0:1, S + tc * 512: S + tc * 512 + n],
                   [MXR.b, ones_r128b.b], PS[4].b, True, True)
                for c in range(nch):
                    tt(ALU.mult, QF.t[:, 0:n], src_fn(c, tc * 512, n),
                       PS[5].t[:, 0:n], [src_bufs[c], PS[5].b], [QF.b])
                    copy(QI.t[:, 0:n], QF.t[:, 0:n], [QF.b], [QI.b])
                    ts(QI.t[:, 0:n], QI.t[:, 0:n], 127, [QI.b], [QI.b],
                       op0=ALU.min, s2=-128, op1=ALU.max)
                    tt(ALU.mult, dst_fn(c, tc * 512, n), QI.t[:, 0:n],
                       PS[4].t[:, 0:n], [QI.b, PS[4].b], [dst_bufs[c]])

        act_quant(xn, XN.bufs, NCH, S, xn, XN.bufs)

        TG = 512
        HH = anew(es2, "HH", [128, 32 * TG], BF16, 32)
        WG = anew(es2, "WG", [128, 8 * 512], BF16, NCH)
        WU = anew(es2, "WU", [128, 8 * 512], BF16, NCH)

        def hh_fn(c, c0, n):
            return HH.t[:, c * TG + c0: c * TG + c0 + n]

        for tg in range(S // TG):
            t0 = tg * TG
            for g0 in range(0, 32, 4):
                for c in range(NCH):
                    load_w_cast(WG.t[:, c * 512: c * 512 + 512], WG[c],
                                wg3q[c * 128:(c + 1) * 128,
                                     g0 * 128:(g0 + 4) * 128])
                    load_w_cast(WU.t[:, c * 512: c * 512 + 512], WU[c],
                                wu3q[c * 128:(c + 1) * 128,
                                     g0 * 128:(g0 + 4) * 128])
                for ml in range(4):
                    m = g0 + ml
                    pg = PS[ml % 2]
                    pu = PS[2 + ml % 2]
                    for c in range(NCH):
                        mm(pg.t[:, :],
                           WG.t[:, c * 512 + ml * 128: c * 512 + (ml + 1) * 128],
                           xn(c, t0, 512), [WG[c], XN[c]],
                           pg.b, c == 0, c == NCH - 1)
                    act(HH.t[:, m * TG: m * TG + 512], pg.t[:, :], AF.Silu,
                        [pg.b], [HH[m]])
                    for c in range(NCH):
                        mm(pu.t[:, :],
                           WU.t[:, c * 512 + ml * 128: c * 512 + (ml + 1) * 128],
                           xn(c, t0, 512), [XN[c], WU[c]],
                           pu.b, c == 0, c == NCH - 1)
                    i2 = ml % 2
                    act(SQ.t[:, i2 * 512:i2 * 512 + 512], pu.t[:, :], AF.Copy,
                        [pu.b], [SQ[0]])
                    tt(ALU.mult, HH.t[:, m * TG: m * TG + 512],
                       HH.t[:, m * TG: m * TG + 512],
                       SQ.t[:, i2 * 512:i2 * 512 + 512], [SQ[0], HH[m]], [HH[m]])
            act_quant(hh_fn, HH.bufs, 32, TG, hh_fn, HH.bufs)
            for m in range(NCH):
                for g0 in range(0, 32, 8):
                    for cc in range(8):
                        load_w_cast(WG.t[:, cc * 512: cc * 512 + 128], WG[cc],
                                    wd3q[(g0 + cc) * 128:(g0 + cc + 1) * 128,
                                         m * 128:(m + 1) * 128])
                    for cc in range(8):
                        ci = g0 + cc
                        mm(PS[m % 3].t[:, :], WG.t[:, cc * 512: cc * 512 + 128],
                           hh_fn(ci, 0, 512), [WG[cc], HH[ci]],
                           PS[m % 3].b, ci == 0, ci == 31)
                act(SQ.t[:, 512:1024], PS[m % 3].t[:, :], AF.Silu, [PS[m % 3].b],
                    [SQ[0]])
                i = m % 2
                dma("sync", SP.t[:, i * 512:(i + 1) * 512],
                    s3scr[m * 128:(m + 1) * 128, t0: t0 + 512],
                    reads=[SSCR.b], writes=[SP.bufs[i]])
                tt(ALU.mult, SQ.t[:, 0:512], SQ.t[:, 512:1024],
                   SP.t[:, i * 512:(i + 1) * 512], [SQ[0], SP.bufs[i]],
                   [SQ[0]])
                tt(ALU.add, xr(m, t0, 512), xr(m, t0, 512), SQ.t[:, 0:512],
                   [SQ[0], XR[m]], [XR[m]])
        es2.close()

    def ffn_update(li):
        es2 = ExitStack()
        AA = anew(es2, "AA", [128, NCH * S], BF16, NCH)
        BB = anew(es2, "BB", [128, NCH * S], BF16, NCH)
        SQ = anew(es2, "SQ", [128, 1024], BF16, 1)
        WQ = anew(es2, "WQ", [128, 8 * 512], BF16, NCH)
        WQ2 = anew(es2, "WQ2", [128, 8 * 512], BF16, NCH)
        cur["AA"], cur["BB"] = AA, BB
        SS = cur["SS"]
        mur = smalls[f"mur{li}"][0]
        murb = smalls[f"mur{li}"][1]
        muk = smalls[f"muk{li}"][0]
        mukb = smalls[f"muk{li}"][1]
        ts(SM2.t[:, 16:24], mur, -1.0, [murb], [SM2[1]], s2=1.0, op1=ALU.add)
        ts(SM2.t[:, 24:32], muk, -1.0, [mukb], [SM2[1]], s2=1.0, op1=ALU.add)

        def r_cb(m, tc, ps):
            act(aa(m, tc * 512, 512), ps.t[:, :], AF.Sigmoid, [ps.b], [AA[m]])

        def k_cb(m, tc, ps):
            act(SQ.t[:, 0:512], ps.t[:, :], AF.Relu, [ps.b], [SQ[0]])
            tt(ALU.mult, bbc(m, tc * 512, 512), SQ.t[:, 0:512], SQ.t[:, 0:512],
               [SQ[0]], [BB[m]])

        proj(ffnW[(li, "r")], 0, 8, r_cb, wpool=WQ,
             scale_col=lambda c: (mur[:, c:c + 1], murb),
             shifted_pair=(ffnW[(li, "r")],
                           lambda c: (SM2.t[:, 16 + c:17 + c], SM2[1])),
             wpool2=WQ2)
        proj(ffnW[(li, "k")], 0, 8, k_cb, wpool=WQ,
             scale_col=lambda c: (muk[:, c:c + 1], mukb),
             shifted_pair=(ffnW[(li, "k")],
                           lambda c: (SM2.t[:, 24 + c:25 + c], SM2[1])),
             wpool2=WQ2)

        def o_cb(m, tc, ps):
            tt(ALU.mult, SQ.t[:, 0:512], ps.t[:, :], aa(m, tc * 512, 512),
               [ps.b, AA[m]], [SQ[0]])
            act(SQ.t[:, 512:1024], SQ.t[:, 0:512], AF.Silu, [SQ[0]], [SQ[0]])
            tt(ALU.mult, SQ.t[:, 0:512], SQ.t[:, 512:1024],
               sschunk(m, tc * 512, 512), [SQ[0], SS[m]], [SQ[0]])
            tt(ALU.add, xr(m, tc * 512, 512), xr(m, tc * 512, 512),
               SQ.t[:, 0:512], [SQ[0], XR[m]], [XR[m]])

        def k2src(c, t0, n):
            return bbc(c, t0, n)

        proj(ffnW[(li, "v")], 0, 8, o_cb, src_fn=k2src, src_bufs=BB.bufs,
             wpool=WQ)
        es2.close()

    def batch(bi, layers):
        for c in range(NCH):
            for piece in range(4):
                i = piece % 2
                dma("sync", WST.t[:, i * 512: i * 512 + 512],
                    x_in[bi, c * 128:(c + 1) * 128, piece * 512:piece * 512 + 512],
                    writes=[WST[i]])
                copy(xr(c, piece * 512, 512), WST.t[:, i * 512: i * 512 + 512],
                     [WST[i]], [XR[c]], eng="gpsimd")
        if 0 in layers:
            layer0(bi)
        if bi == 0:
            quant_weight_to_scratch(wg3T, 8, 32, wg3q, None)
        if 1 in layers:
            layer1(bi)
        if bi == 0:
            quant_weight_to_scratch(wu3T, 8, 32, wu3q, None)
        if 2 in layers:
            layer2(bi)
        if bi == 0:
            quant_weight_to_scratch(wd3T, 32, 8, wd3q, None)
        if 3 in layers:
            layer3(bi)
        for c in range(NCH):
            for piece in range(4):
                i = piece % 2
                copy(WST.t[:, i * 512: i * 512 + 512], xr(c, piece * 512, 512),
                     [XR[c]], [WST[i]], eng="gpsimd")
                dma("sync", y_out[bi, c * 128:(c + 1) * 128,
                                  piece * 512:piece * 512 + 512],
                    WST.t[:, i * 512: i * 512 + 512], reads=[WST[i]])

    import os as _os
    KPHASE = int(_os.environ.get("KPHASE", "99"))
    build_consts()
    load_small("bqkv0", bqkv0, 0)
    load_small("vconv", vconv, 1)
    load_small("mur0", ffnW[(0, "mur")], 2)
    load_small("muk0", ffnW[(0, "muk")], 3)
    load_small("mur1", ffnW[(1, "mur")], 4)
    load_small("bk1", bk1, 6)
    load_small("bv1", bv1, 7)
    load_small("bout2", bout2, 1, pool=SM2)
    load_small("bq3", bq3, 2, pool=SM2)
    load_small("bk3", bk3, 3, pool=SM2)
    load_small("bo3", bo3, 4, pool=SM2)
    load_small("muk1", ffnW[(1, "muk")], 5)

    layers = set(range(n_layers))
    nb = int(_os.environ.get("KBATCH", "2"))
    for bi in range(nb):
        batch(bi, layers)

    print("[build] eng ops:", p.eng_n, "dma:", p.dma_n)
    p.emit(nc, es)
    es.close()
    return nc


def _prep_inputs(kw, core):
    b0 = 2 * core
    x = kw["L0_attn_x"][b0:b0 + 2]
    d = {}
    d["x_cm"] = np.ascontiguousarray(x.transpose(0, 2, 1))
    d["W_qkv0"] = np.ascontiguousarray(kw["L0_attn_W_qkv"])
    d["bqkv0"] = np.ascontiguousarray(kw["L0_attn_b_qkv"].reshape(24, 128).T)
    d["vconv"] = np.ascontiguousarray(
        kw["L0_attn_W_vconv"].reshape(8, 128, 3).transpose(1, 0, 2).reshape(128, 24))
    for li, pre in ((0, "L0_ffn"), (1, "L1_ffn")):
        for nm in ("r", "k", "v"):
            d[f"ffn{li}_W{nm}T"] = np.ascontiguousarray(kw[f"{pre}_W_{nm}"].T)
        d[f"ffn{li}_mur"] = np.ascontiguousarray(kw[f"{pre}_mu_r"].reshape(8, 128).T)
        d[f"ffn{li}_muk"] = np.ascontiguousarray(kw[f"{pre}_mu_k"].reshape(8, 128).T)
    d["Wk1"] = np.ascontiguousarray(kw["L1_attn_Wk"])
    d["Wv1"] = np.ascontiguousarray(kw["L1_attn_Wv"])
    d["bk1"] = np.ascontiguousarray(kw["L1_attn_bk"].reshape(8, 128).T)
    d["bv1"] = np.ascontiguousarray(kw["L1_attn_bv"].reshape(8, 128).T)
    rh = kw["L1_attn_ReadingHead"]
    rmat = np.zeros((1024, 16), np.float32)
    for h in range(16):
        rmat[64 * h:64 * h + 64, h] = rh[h]
    d["RHmat"] = rmat
    d["bidir"] = np.ascontiguousarray(kw["L1_attn_bidirection_weight"].reshape(1, 128))
    sel = np.zeros((16, 1024), np.float32)
    for hp in range(8):
        for h2 in range(2):
            sel[2 * hp + h2, 128 * hp + 64 * h2:128 * hp + 64 * h2 + 64] = 1.0
    d["selHP"] = sel
    wq2 = kw["L2_attn_Wqkv"].copy()
    perm = np.concatenate([np.arange(0, 64, 2), np.arange(1, 64, 2)])
    colperm = np.arange(3072)
    for sec in range(2):
        for h in range(16):
            base = sec * 1024 + h * 64
            colperm[base:base + 64] = base + perm
    d["Wqkv2"] = np.ascontiguousarray(wq2[:, colperm])
    d["Wout2"] = np.ascontiguousarray(kw["L2_attn_Wout"])
    d["bout2"] = np.ascontiguousarray(kw["L2_attn_bout"].reshape(8, 128).T)
    i = np.arange(32, dtype=np.float64)
    theta = np.power(10000.0, -2.0 * (i - 1.0) / 64).astype(np.float64)
    ang = np.arange(2048, dtype=np.float64)[:, None] * theta[None, :]
    cos = np.cos(ang).astype(np.float32)
    sin = np.sin(ang).astype(np.float32)
    cosA = np.zeros((128, 2048), np.float32)
    sinA = np.zeros((128, 2048), np.float32)
    for blk in range(4):
        r0 = 32 * blk
        cosA[r0:r0 + 32] = cos.T
        sinA[r0:r0 + 32] = sin.T * (1.0 if (blk % 2) else -1.0)
    d["cosA"] = cosA
    d["sinA"] = sinA
    P = np.zeros((128, 128), np.float32)
    for m in range(128):
        blk = m // 32
        src = m + 32 if blk % 2 == 0 else m - 32
        P[src, m] = 1.0
    d["Pswap"] = P
    d["wgT2"] = np.ascontiguousarray(kw["L2_ffn_w_gate"].T)
    d["wuT2"] = np.ascontiguousarray(kw["L2_ffn_w_up"].T)
    d["wdT2"] = np.ascontiguousarray(kw["L2_ffn_w_down"].T)
    d["wqT3"] = np.ascontiguousarray(kw["L3_attn_wq"].T)
    d["wkT3"] = np.ascontiguousarray(kw["L3_attn_wk"].T)
    d["wvT3"] = np.ascontiguousarray(kw["L3_attn_wv"].T)
    d["woT3"] = np.ascontiguousarray(kw["L3_attn_wo"].T)
    d["bq3"] = np.ascontiguousarray(kw["L3_attn_bq"].reshape(8, 128).T)
    d["bk3"] = np.ascontiguousarray(kw["L3_attn_bk"].reshape(8, 128).T)
    d["bo3"] = np.ascontiguousarray(kw["L3_attn_bo"].reshape(8, 128).T)
    d["bv3row"] = np.ascontiguousarray(kw["L3_attn_bv"].reshape(1, 1024))
    d["T2"] = np.ascontiguousarray(
        kw["L3_attn_rpe_table"][:, :4095][:, ::-1, :].transpose(0, 2, 1))
    d["wg3T"] = np.ascontiguousarray(kw["L3_ffn_w_gate"].T)
    d["wu3T"] = np.ascontiguousarray(kw["L3_ffn_w_up"].T)
    d["wd3T"] = np.ascontiguousarray(kw["L3_ffn_w_down"].T)
    return d


_nc_cache = {}


def kernel(**kw):
    import os as _os
    n_layers = kw.pop("_n_layers", 4)
    ncores = int(_os.environ.get("KCORES", NCORES))
    if n_layers not in _nc_cache:
        _nc_cache[n_layers] = build_program(n_layers)
    nc = _nc_cache[n_layers]
    in_maps = [_prep_inputs(kw, core) for core in range(ncores)]
    res = run_bass_kernel_spmd(nc, in_maps, core_ids=list(range(ncores)))
    outs = []
    for core in range(ncores):
        y = res.results[core]["y_cm"]
        outs.append(np.ascontiguousarray(y.transpose(0, 2, 1)))
    out = np.concatenate(outs, axis=0).astype(np.float32)
    if ncores < NCORES:
        out = np.concatenate([out] + [out[:2]] * (2 * (NCORES - ncores)))[:16]
    return out
